# version 07
# baseline (speedup 1.0000x reference)
import math
import numpy as np
import ml_dtypes
import concourse.bass as bass
import concourse.mybir as mybir
from concourse.bass_utils import run_bass_kernel_spmd

F32 = mybir.dt.float32
BF16 = mybir.dt.bfloat16
AF = mybir.ActivationFunctionType
ALU = mybir.AluOpType
AX = mybir.AxisListType
BF = ml_dtypes.bfloat16

CFG_FULL = dict(T=1024, D=2048, HD=128, E=16, I=1408, SI=2816, R=8)


def _CONST_VALS(cfg):
    return [1e-5, -0.5, 1.0 / math.sqrt(cfg["HD"]), 1.0, -1.0,
            1.0 / cfg["D"], 1e30, 0.0, 2.0, 0.5]


class Prog:
    ENGINES = ("tensor", "vector", "scalar", "gpsimd", "sync")

    def __init__(self, nc):
        self.nc = nc
        self.ops = {e: [] for e in self.ENGINES}
        self.cnt = {}
        self.bar_floor = 0

    def sem(self, *names):
        for n in names:
            self.cnt[n] = 0

    def emit(self, eng, fn, wait=(), inc=None, amt=1, nobar=False):
        w = [(s, v) for (s, v) in wait if v and v > 0]
        for s, v in w:
            assert v <= self.cnt[s], f"forward wait {s}:{v}>{self.cnt[s]} on {eng}"
        if not nobar and self.bar_floor:
            w.append(("bar", self.bar_floor))
        tgt = None
        if inc is not None:
            self.cnt[inc] += amt
            tgt = self.cnt[inc]
        self.ops[eng].append((fn, w, inc, amt))
        return tgt

    def dma(self, eng, out, in_, inc, wait=()):
        def fn(e, o=out, i=in_):
            return e.dma_start(out=o, in_=i)
        return self.emit(eng, fn, wait=wait, inc=inc, amt=16)

    def barrier(self):
        prev = self.bar_floor
        for e in self.ENGINES:
            if e == "sync":
                fn = lambda eng: eng.nop()
            else:
                fn = lambda eng: eng.drain()
            self.emit(e, fn, wait=([("bar", prev)] if prev else []), inc="bar",
                      nobar=True)
        self.bar_floor = self.cnt["bar"]

    def replay(self, block, sems):
        for name in self.ENGINES:
            ops = self.ops[name]

            def mk(ops=ops):
                def body(eng):
                    last = {}
                    for fn, waits, inc, amt in ops:
                        for s, v in waits:
                            if last.get(s, 0) < v:
                                eng.wait_ge(sems[s], v)
                                last[s] = v
                        ins = fn(eng)
                        if inc is not None:
                            ins.then_inc(sems[inc], amt)
                return body

            getattr(block, name)(mk())


def _treeplan(n):
    steps, cur, tmp = [], [("e", i) for i in range(n)], 0
    while len(cur) > 1:
        nxt = []
        for i in range(0, len(cur) - 1, 2):
            dst = ("s", 0) if len(cur) == 2 else ("t", tmp)
            steps.append((dst, cur[i], cur[i + 1]))
            nxt.append(dst)
            tmp += 1
        if len(cur) % 2:
            nxt.append(cur[-1])
        cur = nxt
    return steps


def build_program(cfg, stop_after=None):
    T, D, HD, E, I, SI, R = (cfg[k] for k in ("T", "D", "HD", "E", "I", "SI", "R"))
    H = D // HD
    TT = T // 128
    DT = D // 128
    IC = I // 128
    KT = T // 128
    SH = SI // I
    NU = SH + E
    Q5 = min(512, T)
    NQ = T // Q5
    RT = 1.0 / math.sqrt(HD)
    assert SI == SH * I and H * HD == D

    nc = bass.Bass(trn_type="TRN2", target_bir_lowering=False)

    x_in = nc.dram_tensor("x", [T, D], F32, kind="ExternalInput")
    c2d = nc.dram_tensor("c2t", [128, T], F32, kind="ExternalInput")
    s2d = nc.dram_tensor("s2t", [128, T], F32, kind="ExternalInput")
    wq4 = nc.dram_tensor("wq4", [DT, 128, D], F32, kind="ExternalInput")
    wk4 = nc.dram_tensor("wk4", [DT, 128, D], F32, kind="ExternalInput")
    wv4 = nc.dram_tensor("wv4", [DT, 128, D], F32, kind="ExternalInput")
    wotd = nc.dram_tensor("wot", [D, D], F32, kind="ExternalInput")
    gate4 = nc.dram_tensor("gate4", [128, DT * E], F32, kind="ExternalInput")
    w1d = nc.dram_tensor("w1t", [NU, IC, 128, DT * 128], BF16, kind="ExternalInput")
    w3d = nc.dram_tensor("w3t", [NU, IC, 128, DT * 128], BF16, kind="ExternalInput")
    w2d = nc.dram_tensor("w2t", [NU, IC, 128, D], BF16, kind="ExternalInput")
    identd = nc.dram_tensor("ident", [128, 128], BF16, kind="ExternalInput")
    permd = nc.dram_tensor("perm", [128, 128], BF16, kind="ExternalInput")
    onehotd = nc.dram_tensor("onehot", [128, H * H], BF16, kind="ExternalInput")
    ones1d = nc.dram_tensor("ones1", [1, 128], F32, kind="ExternalInput")
    cvalsd = nc.dram_tensor("cvals", [128, 16], F32, kind="ExternalInput")
    out_t = nc.dram_tensor("out", [T, D], F32, kind="ExternalOutput")
    kt_in = nc.dram_tensor("kt_in", [D, T], BF16)
    v_in = nc.dram_tensor("v_in", [T, D], BF16)
    aspace = "Shared" if R > 4 else "Local"
    kt_all = nc.dram_tensor("kt_all", [R * D, T], BF16, addr_space=aspace)
    v_all = nc.dram_tensor("v_all", [R * T, D], BF16, addr_space=aspace)

    p = Prog(nc)
    p.sem("bar", "pe", "act", "dve", "pool",
          "dc", "dx0", "dx1", "dxy", "gw0", "gw1", "gw2", "gwf",
          "dk0", "dk1", "dk2", "dk3", "dv0", "dv1",
          "dko0", "dko1", "dvs0", "dvs1",
          "da0", "da1", "db0", "db1", "dw2",
          "dfl", "dout", "cc1", "cc2")

    import contextlib
    st_persist = contextlib.ExitStack()
    SP_ = lambda stk: (lambda *a, **k: stk.enter_context(nc.sbuf_tensor(*a, **k)))
    PP_ = lambda stk: (lambda *a, **k: stk.enter_context(nc.psum_tensor(*a, **k)))
    S0 = SP_(st_persist)

    ident = S0("c_ident", [128, 128], BF16)
    perm = S0("c_perm", [128, 128], BF16)
    onehot = S0("c_onehot", [128, H * H], BF16)
    ones1 = S0("c_ones1", [1, 128], F32)
    c2 = S0("c_c2", [128, T], F32)
    s2 = S0("c_s2", [128, T], F32)
    gate = S0("c_gate", [128, DT * E], BF16)
    stats = S0("stats", [128, 16], F32)
    sgate = S0("sgate", [128, TT * E], F32)
    dwall = S0("dwall", [128, TT * E], F32)
    tke = S0("tke", [128, 5 * E], F32)
    cvals = S0("c_cvals", [128, 16], F32)

    for dst, src in ((ident, identd), (perm, permd), (onehot, onehotd),
                     (c2, c2d), (s2, s2d)):
        p.dma("sync", dst[:, :], src[:, :], inc="dc")
    p.dma("sync", ones1[0:1, :], ones1d[0:1, :], inc="dc")
    p.dma("sync", cvals[:, :], cvalsd[:, :], inc="dc")
    CONST = p.cnt["dc"]
    for ci, cv in enumerate(_CONST_VALS(cfg)):
        nc.const_aps.register(F32, cv, cvals[:, ci:ci + 1])
    for _e in Prog.ENGINES:
        p.emit(_e, lambda e: e.nop(), wait=[("dc", CONST)])
    p.dma("gpsimd", gate[:, :], gate4[:, :], inc="gwf")

    st_right = contextlib.ExitStack()
    qt = st_right.enter_context(nc.sbuf_tensor("qt", [128, H * T], BF16, side="right"))
    at = st_right.enter_context(nc.sbuf_tensor("at", [128, H * T], BF16, side="right"))

    def emit_ln(src_fn, wait_fn, done_fn, lnb, sqs, dst_t, tpsq):
        lnb_war = [0, 0]
        tps_war = [0, 0]
        stats_war = [0, 0]
        ready = {}
        for tt in range(TT):
            sl = tt % 2
            so = sl * 8
            xw = list(wait_fn(tt))
            src = src_fn(tt)
            v1 = p.emit("vector", lambda e, s=src, o=stats[:, so:so + 1]:
                        e.reduce_sum(o, s, axis=AX.X),
                        wait=xw + [("act", stats_war[sl])], inc="dve")
            a2 = p.emit("scalar", lambda e, s=src, o=sqs[:, :], a=stats[:, so + 1:so + 2]:
                        e.activation(o, s, AF.Square, accum_out=a), wait=xw, inc="act")
            v = p.emit("vector", lambda e, i=stats[:, so:so + 1], o=stats[:, so + 2:so + 3]:
                       e.tensor_scalar_mul(o, i, 1.0 / D),
                       wait=[("act", a2), ("dve", v1)], inc="dve")
            v = p.emit("vector", lambda e, i=stats[:, so + 1:so + 2], o=stats[:, so + 3:so + 4]:
                       e.tensor_scalar_mul(o, i, 1.0 / D), wait=[("dve", v)], inc="dve")
            v = p.emit("vector", lambda e, m=stats[:, so + 2:so + 3], o=stats[:, so + 4:so + 5]:
                       e.tensor_mul(o, m, m), wait=[("dve", v)], inc="dve")
            vv = p.emit("vector", lambda e, a=stats[:, so + 3:so + 4],
                        b=stats[:, so + 4:so + 5], o=stats[:, so + 5:so + 6]:
                        e.tensor_sub(o, a, b), wait=[("dve", v)], inc="dve")
            al = p.emit("scalar", lambda e, i=stats[:, so + 5:so + 6], o=stats[:, so + 6:so + 7]:
                        e.activation(o, i, AF.Ln, bias=1e-5), wait=[("dve", vv)], inc="act")
            ar = p.emit("scalar", lambda e, i=stats[:, so + 6:so + 7], o=stats[:, so + 7:so + 8]:
                        e.activation(o, i, AF.Exp, scale=-0.5), wait=[("act", al)], inc="act")
            v = p.emit("vector", lambda e, m=stats[:, so + 2:so + 3], r=stats[:, so + 7:so + 8],
                       o=stats[:, so + 4:so + 5]:
                       e.tensor_mul(o, m, r), wait=[("act", ar), ("dve", vv)], inc="dve")
            vnb = p.emit("vector", lambda e, i=stats[:, so + 4:so + 5], o=stats[:, so + 4:so + 5]:
                         e.tensor_scalar_mul(o, i, -1.0), wait=[("dve", v)], inc="dve")
            alnb = p.emit("scalar",
                          lambda e, s=src, o=lnb[:, sl * D:(sl + 1) * D],
                          sc=stats[:, so + 7:so + 8], b=stats[:, so + 4:so + 5]:
                          e.activation(o, s, AF.Identity, scale=sc, bias=b),
                          wait=[("dve", vnb), ("act", ar), ("pe", lnb_war[sl])], inc="act")
            stats_war[sl] = alnb
            done_fn(tt, [("act", alnb), ("dve", v1)])
            for dt in range(DT):
                j = (tt * DT + dt) % 2
                ttr = p.emit("tensor",
                             lambda e, i=lnb[:, sl * D + dt * 128: sl * D + (dt + 1) * 128],
                             o=tpsq[j][:, :]:
                             e.transpose(o, i, ident[:, :]),
                             wait=[("act", alnb), ("dve", tps_war[j]), ("dc", CONST)],
                             inc="pe")
                if dt == DT - 1:
                    lnb_war[sl] = ttr
                tps_war[j] = p.emit("vector",
                                    lambda e, i=tpsq[j][:, :],
                                    o=dst_t[:, dt * T + tt * 128: dt * T + (tt + 1) * 128]:
                                    e.tensor_copy(o, i), wait=[("pe", ttr)], inc="dve")
                ready[(dt, tt)] = tps_war[j]
        return ready

    st_A = contextlib.ExitStack()
    SA = SP_(st_A)
    ln1t = SA("ln1t", [128, DT * T], BF16)
    xb = SA("xb", [128, 2 * D], F32)
    lnb1 = SA("lnb1", [128, 2 * D], BF16)
    sqs1 = SA("sqs1", [128, D], BF16)
    st_ps0 = contextlib.ExitStack()
    tps0 = [PP_(st_ps0)("tps0a", [128, 128], BF16),
            PP_(st_ps0)("tps0b", [128, 128], BF16)]

    x_tgt = {}
    x_tgt[0] = p.dma("sync", xb[:, 0:D], x_in[0:128, :], inc="dx0")
    if TT > 1:
        x_tgt[1] = p.dma("sync", xb[:, D:2 * D], x_in[128:256, :], inc="dx1")

    def x_done(tt, tgts):
        if tt + 2 < TT:
            x_tgt[tt + 2] = p.dma("sync", xb[:, (tt % 2) * D:(tt % 2 + 1) * D],
                                  x_in[(tt + 2) * 128:(tt + 3) * 128, :],
                                  inc="dx%d" % (tt % 2), wait=tgts)

    ln1_ready = emit_ln(lambda tt: xb[:, (tt % 2) * D:(tt % 2 + 1) * D],
                        lambda tt: [("dx%d" % (tt % 2), x_tgt[tt])],
                        x_done, lnb1, sqs1, ln1t, tps0)
    LN1_ALL = max(ln1_ready.values())
    p.barrier()
    st_ps0.close()

    def early_out(src_sb):
        p.barrier()
        eng = "sync" if src_sb.tensor.dtype == F32 else "gpsimd"
        for tt in range(TT):
            p.dma(eng, out_t[tt * 128:(tt + 1) * 128, :], src_sb,
                  inc="dout", wait=[])
        p.emit("sync", lambda e: e.nop(), wait=[("dout", p.cnt["dout"])])
        with contextlib.ExitStack() as rs:
            sems = {}
            for name in p.cnt.keys():
                sems[name] = rs.enter_context(nc.semaphore("sem_" + name))
            block = rs.enter_context(nc.Block())
            p.replay(block, sems)
        return nc

    if stop_after == "p0":
        return early_out(ln1t[:, 0:D])

    st_P1 = contextlib.ExitStack()
    S1 = SP_(st_P1)
    wb = S1("wb", [128, 3 * D], BF16)
    qe = S1("qe", [128, 3 * T], BF16)
    t1b = S1("t1b", [128, T], F32)
    t2b = S1("t2b", [128, T], F32)
    kst = S1("kst", [128, 2 * T], BF16)
    vst = S1("vst", [128, 2 * T], BF16)
    st_ps1 = contextlib.ExitStack()
    pps = PP_(st_ps1)("pps", [128, 2 * T], F32)
    st_ps1qk = contextlib.ExitStack()
    sps = PP_(st_ps1qk)("sps", [128, 2 * T], F32)

    wb_war = [0, 0, 0]
    pps_war = [[], []]
    sps_war = [[], []]
    qe_war = [[], [], []]
    t12_war = [0]
    kst_war = [0, 0]
    vst_war = [0, 0]
    vtb_war = [0, 0, 0]

    def proj(kind, w4, limit=None):
        for ot in range(DT):
            sl = ot % 2
            sl3 = ot % 3
            ld = p.dma("gpsimd", wb[:, sl3 * D:(sl3 + 1) * D], w4[ot, :, :],
                       inc="gw%d" % sl3, wait=[("pe", wb_war[sl3])])
            mm_stop = None
            for qn in range(NQ):
                for dt in range(DT):
                    w = [("gw%d" % sl3, ld), ("dve", LN1_ALL)]
                    if dt == 0 and qn == 0:
                        w += pps_war[sl]
                    mm_stop = p.emit(
                        "tensor",
                        lambda e, o=pps[:, sl * T + qn * Q5: sl * T + qn * Q5 + Q5],
                        l=wb[:, sl3 * D + dt * 128: sl3 * D + (dt + 1) * 128],
                        r=ln1t[:, dt * T + qn * Q5: dt * T + qn * Q5 + Q5],
                        s0=(dt == 0), s1=(dt == DT - 1):
                        e.matmul(o, l, r, start=s0, stop=s1),
                        wait=w, inc="pe")
            wb_war[sl3] = mm_stop
            pslice = pps[:, sl * T:(sl + 1) * T]
            if kind in ("q", "k"):
                aev = p.emit("scalar", lambda e, i=pslice, o=qe[:, sl3 * T:(sl3 + 1) * T]:
                             e.activation(o, i, AF.Copy),
                             wait=[("pe", mm_stop)] + qe_war[sl3], inc="act")
                if limit == "mm":
                    pps_war[sl] = [("act", aev)]
                    qe_war[sl3] = [("pe", mm_stop)]
                    continue
                sw_stop = None
                for qn in range(NQ):
                    w = [("act", aev)]
                    if qn == 0:
                        w += sps_war[sl]
                    sw_stop = p.emit(
                        "tensor",
                        lambda e, o=sps[:, sl * T + qn * Q5: sl * T + qn * Q5 + Q5],
                        r=qe[:, sl3 * T + qn * Q5: sl3 * T + qn * Q5 + Q5]:
                        e.matmul(o, perm[:, :], r, start=True, stop=True),
                        wait=w, inc="pe")
                qe_war[sl3] = [("pe", sw_stop)]
                if limit == "swap":
                    pps_war[sl] = [("act", aev)]
                    sps_war[sl] = []
                    continue
                v1 = p.emit("vector", lambda e, i=qe[:, sl3 * T:(sl3 + 1) * T]:
                            e.tensor_mul(t1b[:, :], i, c2[:, :]),
                            wait=[("act", aev), ("dve", t12_war[0])], inc="dve")
                v2 = p.emit("vector", lambda e, i=sps[:, sl * T:(sl + 1) * T]:
                            e.tensor_mul(t2b[:, :], i, s2[:, :]),
                            wait=[("pe", sw_stop), ("dve", t12_war[0])], inc="dve")
                pps_war[sl] = [("act", aev)]
                sps_war[sl] = [("dve", v2)]
                qe_war[sl3] = [("pe", sw_stop), ("dve", v1)]
                if limit == "rope1":
                    t12_war[0] = v2
                    continue
                if kind == "q":
                    dst = qt[:, ot * T:(ot + 1) * T]
                    vo = p.emit("vector", lambda e, d=dst:
                                e.tensor_add(d, t1b[:, :], t2b[:, :]),
                                wait=[("dve", v1), ("dve", v2)], inc="dve")
                    t12_war[0] = vo
                else:
                    dst = kst[:, sl * T:(sl + 1) * T]
                    vo = p.emit("vector", lambda e, d=dst:
                                e.tensor_add(d, t1b[:, :], t2b[:, :]),
                                wait=[("dve", v1), ("dve", v2),
                                      ("dko%d" % sl, kst_war[sl])], inc="dve")
                    t12_war[0] = vo
                    kst_war[sl] = p.dma("sync", kt_in[ot * 128:(ot + 1) * 128, :],
                                        dst, inc="dko%d" % sl, wait=[("dve", vo)])
            else:
                aev = p.emit("scalar", lambda e, i=pslice, o=qe[:, sl3 * T:(sl3 + 1) * T]:
                             e.activation(o, i, AF.Copy),
                             wait=[("pe", mm_stop)] + qe_war[sl3], inc="act")
                pps_war[sl] = [("act", aev)]
                last_cp = None
                for tc in range(TT):
                    j = (ot * TT + tc) % 3
                    ttr = p.emit("tensor",
                                 lambda e, i=qe[:, sl3 * T + tc * 128: sl3 * T + (tc + 1) * 128],
                                 o=vtp[j][:, :]:
                                 e.transpose(o, i, ident[:, :]),
                                 wait=[("act", aev), ("dve", vtb_war[j])], inc="pe")
                    if tc == TT - 1:
                        qe_war[sl3] = [("pe", ttr)]
                    w = [("pe", ttr)]
                    if tc == 0:
                        w.append(("dvs%d" % sl, vst_war[sl]))
                    last_cp = p.emit("vector",
                                     lambda e, i=vtp[j][:, :],
                                     o=vst[:, sl * T + tc * 128: sl * T + (tc + 1) * 128]:
                                     e.tensor_copy(o, i), wait=w, inc="dve")
                    vtb_war[j] = last_cp
                dst_ap = v_in[:, ot * 128:(ot + 1) * 128].rearrange(
                    "(tc p) hd -> p tc hd", p=128)
                src_ap = vst[:, sl * T:(sl + 1) * T].rearrange(
                    "p (tc hd) -> p tc hd", hd=128)
                vst_war[sl] = p.dma("sync", dst_ap, src_ap,
                                    inc="dvs%d" % sl, wait=[("dve", last_cp)])

    if stop_after in ("p1q_mm", "p1q_swap", "p1q_rope1"):
        proj("q", wq4, limit=stop_after[4:])
        return early_out(qe[:, 0:D])
    proj("q", wq4)
    if stop_after == "p1q":
        return early_out(qt[:, 0:D])
    proj("k", wk4)
    k_wait = [(s, p.cnt[s]) for s in ("dko0", "dko1")]
    cc1t = p.emit("gpsimd", lambda e: e.collective_compute(
        "AllGather", ALU.bypass, replica_groups=[list(range(R))],
        ins=[kt_in.ap().opt()], outs=[kt_all.ap().opt()]),
        wait=k_wait, inc="cc1", amt=1)
    p.barrier()
    st_ps1qk.close()
    st_ps1v = contextlib.ExitStack()
    vtp = [PP_(st_ps1v)("vtpa", [128, 128], BF16),
           PP_(st_ps1v)("vtpb", [128, 128], BF16),
           PP_(st_ps1v)("vtpc", [128, 128], BF16)]
    proj("v", wv4)
    if stop_after == "p1":
        return early_out(qt[:, 0:D])
    v_wait = [(s, p.cnt[s]) for s in ("dvs0", "dvs1")]
    cc2t = p.emit("gpsimd", lambda e: e.collective_compute(
        "AllGather", ALU.bypass, replica_groups=[list(range(R))],
        ins=[v_in.ap().opt()], outs=[v_all.ap().opt()]),
        wait=v_wait, inc="cc2", amt=1)

    if stop_after == "cc":
        return early_out(qt[:, 0:D])
    p.barrier()
    st_ps1v.close()
    st_ps1.close()
    st_P1.close()
    st_A.close()

    st_P3 = contextlib.ExitStack()
    S3 = SP_(st_P3)
    ktb = S3("ktb", [128, 4 * T], BF16)
    vb = S3("vb", [128, 2 * KT * D], BF16)
    expt = S3("expt", [128, 2 * KT * T], BF16)
    tplan = _treeplan(KT)
    ntmp = sum(1 for d, _, _ in tplan if d[0] == "t")
    tbuf = S3("tbuf", [128, max(1, ntmp) * T], BF16)
    sch = S3("sch", [128, 2 * T], BF16)
    st_ps3 = contextlib.ExitStack()
    scp = PP_(st_ps3)("scp", [128, 2 * T], F32)
    pvp = PP_(st_ps3)("pvp", [128, T], F32)
    psS = PP_(st_ps3)("psS", [H, T], F32)

    ktb_war = [0, 0, 0, 0]
    vb_war = [0, 0]
    kt_tgt = {}
    v_tgt = {}

    def load_kt(m):
        r, h = divmod(m, H)
        kt_tgt[m] = p.dma("sync", ktb[:, (m % 4) * T:(m % 4 + 1) * T],
                          kt_all[r * D + h * 128: r * D + (h + 1) * 128, :],
                          inc="dk%d" % (m % 4),
                          wait=[("pe", ktb_war[m % 4]), ("cc1", 1)])

    def load_v(r):
        src = v_all[r * T:(r + 1) * T, :].rearrange("(kt p) d -> p kt d", p=128)
        dst = vb[:, (r % 2) * KT * D:(r % 2 + 1) * KT * D].rearrange(
            "p (kt d) -> p kt d", d=D)
        v_tgt[r] = p.dma("sync", dst, src,
                         inc="dv%d" % (r % 2),
                         wait=[("pe", vb_war[r % 2]), ("cc2", 1)])

    load_v(0)
    for m0 in range(min(4, R * H)):
        load_kt(m0)

    scp_war = [0, 0]
    pvp_war = []
    expt_war = [[], []]
    sch_war = [0, 0]
    smm_last = 0
    at_last = [0] * H
    pend_dsum = None
    ndsum = [0]

    def emit_dsum(mm_, hh, wlist):
        nonlocal smm_last
        for qn in range(NQ):
            smm_last = p.emit(
                "tensor",
                lambda e, o=psS[0:H, qn * Q5:(qn + 1) * Q5],
                l=onehot[:, hh * H:(hh + 1) * H],
                r_=sch[:, (mm_ % 2) * T + qn * Q5:(mm_ % 2) * T + qn * Q5 + Q5],
                s0=(ndsum[0] == 0), s1=(ndsum[0] == R * H - 1):
                e.matmul(o, l, r_, start=s0, stop=s1),
                wait=wlist, inc="pe")
        ndsum[0] += 1
        sch_war[mm_ % 2] = smm_last

    for m in range(R * H):
        r, h = divmod(m, H)
        s2_ = m % 2
        if h == 0 and r + 1 < R:
            load_v(r + 1)
        ex_t = {}
        pv_stop = None

        def emit_sc(kt):
            j = kt % 2
            mm = None
            for qn in range(NQ):
                w = [("dk%d" % (m % 4), kt_tgt[m]), ("act", scp_war[j])]
                mm = p.emit("tensor",
                            lambda e, o=scp[:, j * T + qn * Q5: j * T + qn * Q5 + Q5],
                            l=ktb[:, (m % 4) * T + kt * 128:(m % 4) * T + (kt + 1) * 128],
                            r_=qt[:, h * T + qn * Q5: h * T + qn * Q5 + Q5]:
                            e.matmul(o, l, r_, start=True, stop=True),
                            wait=w, inc="pe")
            w2 = [("pe", mm)]
            if kt == 0:
                w2 += expt_war[s2_]
            ex_t[kt] = p.emit(
                "scalar",
                lambda e, i=scp[:, j * T:(j + 1) * T],
                o=expt[:, s2_ * KT * T + kt * T: s2_ * KT * T + (kt + 1) * T]:
                e.activation(o, i, AF.Exp, scale=RT), wait=w2, inc="act")
            scp_war[j] = ex_t[kt]

        def emit_pv(kt):
            nonlocal pv_stop
            for qn in range(NQ):
                w = [("act", ex_t[kt]), ("dv%d" % (r % 2), v_tgt[r])]
                if kt == 0 and qn == 0:
                    w += pvp_war
                pv_stop = p.emit(
                    "tensor",
                    lambda e, o=pvp[:, qn * Q5: qn * Q5 + Q5],
                    l=vb[:, (r % 2) * KT * D + kt * D + h * 128:
                          (r % 2) * KT * D + kt * D + (h + 1) * 128],
                    r_=expt[:, s2_ * KT * T + kt * T + qn * Q5:
                            s2_ * KT * T + kt * T + qn * Q5 + Q5],
                    s0=(kt == 0), s1=(kt == KT - 1):
                    e.matmul(o, l, r_, start=s0, stop=s1),
                    wait=w, inc="pe")

        emit_sc(0)
        if KT > 1:
            emit_sc(1)
        for kt in range(KT):
            if kt + 2 < KT:
                emit_sc(kt + 2)
            emit_pv(kt)
            if kt == 0 and pend_dsum is not None:
                emit_dsum(*pend_dsum)
                pend_dsum = None
        if pend_dsum is not None:
            emit_dsum(*pend_dsum)
            pend_dsum = None
        ktb_war[m % 4] = pv_stop
        if h == H - 1:
            vb_war[r % 2] = pv_stop
        node_t = {}
        lastv = None
        lastp = 0
        for si, (dst, a, b) in enumerate(tplan):
            def ap_of(n):
                k, i = n
                if k == "e":
                    return expt[:, s2_ * KT * T + i * T:(s2_ * KT * T + (i + 1) * T)]
                if k == "t":
                    return tbuf[:, i * T:(i + 1) * T]
                return sch[:, (m % 2) * T:(m % 2 + 1) * T]
            eng = "vector"
            w = []
            for n in (a, b):
                if n[0] == "e":
                    w.append(("act", ex_t[n[1]]))
                elif n in node_t:
                    w.append(node_t[n])
            if dst[0] == "s":
                w.append(("pe", sch_war[m % 2]))
            tgt = p.emit(eng, lambda e, o=ap_of(dst), x=ap_of(a), y=ap_of(b):
                         e.tensor_add(o, x, y), wait=w, inc="dve")
            node_t[dst] = ("dve", tgt)
            lastv = tgt
        pend_dsum = (m, h, [node_t[("s", 0)]])
        ew = [("pe", pv_stop)]
        if lastv:
            ew.append(("dve", lastv))
        expt_war[s2_] = ew
        if r == 0:
            vac = p.emit("vector", lambda e, i=pvp[:, :],
                         o=at[:, h * T:(h + 1) * T]:
                         e.tensor_copy(o, i), wait=[("pe", pv_stop)], inc="dve")
        else:
            vac = p.emit("vector", lambda e, i=pvp[:, :],
                         o=at[:, h * T:(h + 1) * T]:
                         e.tensor_add(o, o, i),
                         wait=[("pe", pv_stop), ("dve", at_last[h])], inc="dve")
        at_last[h] = vac
        pvp_war = [("dve", vac)]
        if m + 4 < R * H:
            load_kt(m + 4)

    if stop_after == "p3":
        return early_out(at[:, 0:D])

    if pend_dsum is not None:
        emit_dsum(*pend_dsum)

    p.barrier()
    st_P3.close()
    st_P4 = contextlib.ExitStack()
    S4 = SP_(st_P4)
    sums = S4("sums", [H, T], F32)
    rsums = S4("rsums", [H, T], F32)
    rsflat = S4("rsflat", [1, H * T], F32)
    vsc = p.emit("vector", lambda e: e.tensor_copy(sums[0:H, :], psS[0:H, :]),
                 wait=[("pe", smm_last)], inc="dve")
    vrc = p.emit("vector", lambda e: e.reciprocal(rsums[0:H, :], sums[0:H, :]),
                 wait=[("dve", vsc)], inc="dve")
    p.barrier()
    st_ps3.close()
    st_ps4 = contextlib.ExitStack()
    rps = PP_(st_ps4)("rps", [128, 2 * T], F32)
    at_sc_h = {}
    fl = p.dma("sync", rsflat[0:1, :].rearrange("a (h t) -> a h t", h=H),
               rsums[0:H, :],
               inc="dfl", wait=[("dve", vrc)])
    rps_war = [0, 0]
    for h in range(H):
        r_stop = None
        for qn in range(NQ):
            w = [("dfl", fl)]
            if qn == 0:
                w.append(("dve", rps_war[h % 2]))
            r_stop = p.emit("tensor",
                            lambda e, o=rps[:, (h % 2) * T + qn * Q5:
                                            (h % 2) * T + qn * Q5 + Q5],
                            r_=rsflat[0:1, h * T + qn * Q5: h * T + qn * Q5 + Q5]:
                            e.matmul(o, ones1[0:1, :], r_, start=True, stop=True),
                            wait=w, inc="pe")
        rps_war[h % 2] = p.emit("vector",
                                lambda e, o=at[:, h * T:(h + 1) * T],
                                i=rps[:, (h % 2) * T:(h % 2 + 1) * T]:
                                e.tensor_mul(o, o, i),
                                wait=[("pe", r_stop), ("dve", at_last[h])], inc="dve")
        at_last[h] = rps_war[h % 2]
        at_sc_h[h] = rps_war[h % 2]
    AT_SCALED = p.cnt["dve"]
    p.barrier()
    st_ps4.close()
    st_P4.close()

    if stop_after == "p4":
        return early_out(at[:, 0:D])

    st_y1 = contextlib.ExitStack()
    y1 = SP_(st_y1)("y1", [128, TT * D], F32)
    st_P5 = contextlib.ExitStack()
    wo_sb = SP_(st_P5)("wo_sb", [128, DT * D], BF16)
    OH = 2 if D >= 1024 else 1
    OW = D // OH
    O5 = min(512, OW)
    NO5 = OW // O5
    st_ps5 = contextlib.ExitStack()
    wop = PP_(st_ps5)("wop", [128, 2 * OW], F32)

    for oh in range(OH):
        for dt in range(DT):
            p.dma("gpsimd", wo_sb[:, dt * D + oh * OW: dt * D + (oh + 1) * OW],
                  wotd[dt * 128:(dt + 1) * 128, oh * OW:(oh + 1) * OW], inc="gwf")
    WO_ALL = p.cnt["gwf"]
    y1_attn = {}
    wop_war = [0, 0]
    for tt in range(TT):
        p.dma("sync", y1[:, tt * D:(tt + 1) * D],
              x_in[tt * 128:(tt + 1) * 128, :], inc="dxy")
    XY_ALL = p.cnt["dxy"]
    for tt in range(TT):
        for oh in range(OH):
            sl = (tt * OH + oh) % 2
            mm_stop = None
            for n5 in range(NO5):
                for dt in range(DT):
                    w = [("gwf", WO_ALL), ("dve", at_sc_h[dt])]
                    if dt == 0 and n5 == 0:
                        w += [("dve", wop_war[sl])]
                    mm_stop = p.emit(
                        "tensor",
                        lambda e, o=wop[:, sl * OW + n5 * O5: sl * OW + n5 * O5 + O5],
                        l=at[:, dt * T + tt * 128: dt * T + (tt + 1) * 128],
                        r_=wo_sb[:, dt * D + oh * OW + n5 * O5:
                                 dt * D + oh * OW + n5 * O5 + O5],
                        s0=(dt == 0), s1=(dt == DT - 1):
                        e.matmul(o, l, r_, start=s0, stop=s1),
                        wait=w, inc="pe")
            wop_war[sl] = p.emit(
                "vector",
                lambda e, o=y1[:, tt * D + oh * OW: tt * D + (oh + 1) * OW],
                i=wop[:, sl * OW:(sl + 1) * OW]:
                e.tensor_add(o, o, i),
                wait=[("pe", mm_stop), ("dxy", XY_ALL)], inc="dve")
        y1_attn[tt] = wop_war[(tt * OH + OH - 1) % 2]
    p.barrier()
    st_ps5.close()
    st_P5.close()
    st_right.close()

    if stop_after == "p5":
        return early_out(y1[:, 0:D])

    st_P6 = contextlib.ExitStack()
    S6 = SP_(st_P6)
    ln2t = SP_(st_y1)("ln2t", [128, DT * T], BF16)
    lnb2 = S6("lnb2", [128, 2 * D], BF16)
    sqs2 = S6("sqs2", [128, D], BF16)
    st_ps6 = contextlib.ExitStack()
    tps6 = [PP_(st_ps6)("tps6a", [128, 128], BF16),
            PP_(st_ps6)("tps6b", [128, 128], BF16)]
    gps = [PP_(st_ps6)("gpsa", [128, E], F32),
           PP_(st_ps6)("gpsb", [128, E], F32)]

    ln2_ready = emit_ln(lambda tt: y1[:, tt * D:(tt + 1) * D],
                        lambda tt: [("dve", y1_attn[tt])],
                        lambda tt, tg: None, lnb2, sqs2, ln2t, tps6)
    gps_war = [0, 0]
    dw_tgt = {}
    tke_war = [0]
    for tt in range(TT):
        sl = tt % 2
        g_stop = None
        for dt in range(DT):
            w = [("dve", ln2_ready[(dt, tt)]), ("gwf", p.cnt["gwf"])]
            if dt == 0:
                w.append(("act", gps_war[sl]))
            g_stop = p.emit("tensor",
                            lambda e, o=gps[sl][:, :],
                            l=ln2t[:, dt * T + tt * 128: dt * T + (tt + 1) * 128],
                            r_=gate[:, dt * E:(dt + 1) * E],
                            s0=(dt == 0), s1=(dt == DT - 1):
                            e.matmul(o, l, r_, start=s0, stop=s1),
                            wait=w, inc="pe")
        asg = p.emit("scalar", lambda e, i=gps[sl][:, :],
                     o=sgate[:, tt * E:(tt + 1) * E]:
                     e.activation(o, i, AF.Sigmoid),
                     wait=[("pe", g_stop)], inc="act")
        gps_war[sl] = asg
        sg = sgate[:, tt * E:(tt + 1) * E]
        v = p.emit("vector", lambda e, i=sg: e.reduce_max(tke[:, 4 * E:4 * E + 1], i, axis=AX.X),
                   wait=[("act", asg), ("dve", tke_war[0])], inc="dve")
        v = p.emit("vector", lambda e, i=sg: e.tensor_scalar(
            tke[:, 0:E], i, tke[:, 4 * E:4 * E + 1], None, ALU.is_ge),
            wait=[("dve", v)], inc="dve")
        v = p.emit("vector", lambda e: e.tensor_scalar(
            tke[:, E:2 * E], tke[:, 0:E], 1e30, None, ALU.mult),
            wait=[("dve", v)], inc="dve")
        v = p.emit("vector", lambda e, i=sg: e.tensor_sub(
            tke[:, 2 * E:3 * E], i, tke[:, E:2 * E]), wait=[("dve", v)], inc="dve")
        v = p.emit("vector", lambda e: e.reduce_max(
            tke[:, 4 * E + 1:4 * E + 2], tke[:, 2 * E:3 * E], axis=AX.X),
            wait=[("dve", v)], inc="dve")
        v = p.emit("vector", lambda e: e.tensor_scalar(
            tke[:, E:2 * E], tke[:, 2 * E:3 * E], tke[:, 4 * E + 1:4 * E + 2],
            None, ALU.is_ge), wait=[("dve", v)], inc="dve")
        v = p.emit("vector", lambda e: e.tensor_add(
            tke[:, 3 * E:4 * E], tke[:, 0:E], tke[:, E:2 * E]),
            wait=[("dve", v)], inc="dve")
        v = p.emit("vector", lambda e, i=sg: e.tensor_mul(
            tke[:, 0:E], tke[:, 3 * E:4 * E], i), wait=[("dve", v)], inc="dve")
        v = p.emit("vector", lambda e: e.tensor_add(
            tke[:, 4 * E + 2:4 * E + 3], tke[:, 4 * E:4 * E + 1],
            tke[:, 4 * E + 1:4 * E + 2]), wait=[("dve", v)], inc="dve")
        v = p.emit("vector", lambda e: e.reciprocal(
            tke[:, 4 * E + 3:4 * E + 4], tke[:, 4 * E + 2:4 * E + 3]),
            wait=[("dve", v)], inc="dve")
        dw_tgt[tt] = p.emit("vector", lambda e, o=dwall[:, tt * E:(tt + 1) * E]:
                            e.tensor_scalar(o, tke[:, 0:E], tke[:, 4 * E + 3:4 * E + 4],
                                            None, ALU.mult), wait=[("dve", v)], inc="dve")
        tke_war[0] = dw_tgt[tt]
    p.barrier()
    st_ps6.close()
    st_P6.close()

    if stop_after == "p6":
        return early_out(y1[:, 0:D])

    st_P7 = contextlib.ExitStack()
    S7 = SP_(st_P7)
    ht = S7("ht", [128, IC * T], BF16)
    s1a = S7("s1a", [128, 2 * D], BF16)
    s1b = S7("s1b", [128, 2 * D], BF16)
    w2sb = S7("w2sb", [128, IC * D], BF16)
    silu = S7("silu", [128, T], F32)
    OW2 = min(1024, D)
    NO2 = D // OW2
    st_ps7 = contextlib.ExitStack()
    aps = PP_(st_ps7)("aps", [128, T], F32)
    bps = PP_(st_ps7)("bps", [128, T], F32)
    yps = PP_(st_ps7)("yps", [128, 2 * OW2], F32)

    s1a_war = [0, 0]
    s1b_war = [0, 0]
    aps_war = 0
    bps_war = 0
    ht_war = {}
    ht_tgt = {}
    w2_tgt = {}
    wy_war = [0, 0]
    y_final = dict((tt, 0) for tt in range(TT))
    stage2_last_mm = 0

    def w2loads(u, wait_pe):
        for ic in range(IC):
            p.dma("gpsimd", w2sb[:, ic * D:(ic + 1) * D],
                  w2d[u, ic, :, :], inc="dw2",
                  wait=([("pe", wait_pe)] if ic == 0 else []))
        w2_tgt[u] = p.cnt["dw2"]

    def stage1_chunk(u, ic):
        nonlocal aps_war, bps_war
        sl = ic % 2
        la = p.dma("sync", s1a[:, sl * D:(sl + 1) * D], w1d[u, ic, :, :],
                   inc="da%d" % sl, wait=[("pe", s1a_war[sl])])
        lb = p.dma("sync", s1b[:, sl * D:(sl + 1) * D], w3d[u, ic, :, :],
                   inc="db%d" % sl, wait=[("pe", s1b_war[sl])])
        a_stop = b_stop = None
        for qn in range(NQ):
            for dt in range(DT):
                w = [("da%d" % sl, la), ("dve", ln2_ready[(dt, TT - 1)])]
                if dt == 0 and qn == 0:
                    w.append(("act", aps_war))
                a_stop = p.emit(
                    "tensor",
                    lambda e, o=aps[:, qn * Q5: qn * Q5 + Q5],
                    l=s1a[:, sl * D + dt * 128: sl * D + (dt + 1) * 128],
                    r_=ln2t[:, dt * T + qn * Q5: dt * T + qn * Q5 + Q5],
                    s0=(dt == 0), s1=(dt == DT - 1):
                    e.matmul(o, l, r_, start=s0, stop=s1), wait=w, inc="pe")
        s1a_war[sl] = a_stop
        for qn in range(NQ):
            for dt in range(DT):
                w = [("db%d" % sl, lb)]
                if dt == 0 and qn == 0:
                    w.append(("dve", bps_war))
                b_stop = p.emit(
                    "tensor",
                    lambda e, o=bps[:, qn * Q5: qn * Q5 + Q5],
                    l=s1b[:, sl * D + dt * 128: sl * D + (dt + 1) * 128],
                    r_=ln2t[:, dt * T + qn * Q5: dt * T + qn * Q5 + Q5],
                    s0=(dt == 0), s1=(dt == DT - 1):
                    e.matmul(o, l, r_, start=s0, stop=s1), wait=w, inc="pe")
        s1b_war[sl] = b_stop
        aps_war = p.emit("scalar", lambda e: e.activation(silu[:, :], aps[:, :], AF.Silu),
                         wait=[("pe", a_stop), ("dve", bps_war)], inc="act")
        bps_war = p.emit("vector", lambda e, o=ht[:, ic * T:(ic + 1) * T]:
                         e.tensor_mul(o, silu[:, :], bps[:, :]),
                         wait=[("act", aps_war), ("pe", b_stop),
                               ("pe", ht_war.get(u - 1, 0))], inc="dve")
        ht_tgt[(u, ic)] = bps_war

    def stage2_chunk(u, tt, oh, sl):
        nonlocal stage2_last_mm
        e_idx = u - SH
        W5 = min(Q5, OW2)
        N5 = OW2 // W5
        mm_stop = None
        for n5 in range(N5):
            for ic in range(IC):
                w = [("dve", ht_tgt[(u, ic)]), ("dw2", w2_tgt[u])]
                if ic == 0 and n5 == 0:
                    w.append(("dve", wy_war[sl]))
                mm_stop = p.emit(
                    "tensor",
                    lambda e, o=yps[:, sl * OW2 + n5 * W5: sl * OW2 + n5 * W5 + W5],
                    l=ht[:, ic * T + tt * 128: ic * T + (tt + 1) * 128],
                    r_=w2sb[:, ic * D + oh * OW2 + n5 * W5:
                            ic * D + oh * OW2 + n5 * W5 + W5],
                    s0=(ic == 0), s1=(ic == IC - 1):
                    e.matmul(o, l, r_, start=s0, stop=s1), wait=w, inc="pe")
        stage2_last_mm = mm_stop
        ysl = y1[:, tt * D + oh * OW2: tt * D + (oh + 1) * OW2]
        if e_idx < 0:
            wy_war[sl] = p.emit("vector",
                                lambda e, o=ysl, i=yps[:, sl * OW2:(sl + 1) * OW2]:
                                e.tensor_add(o, o, i),
                                wait=[("pe", mm_stop), ("dve", y_final[tt])], inc="dve")
        else:
            wy_war[sl] = p.emit(
                "vector",
                lambda e, o=ysl, i=yps[:, sl * OW2:(sl + 1) * OW2],
                dw=dwall[:, tt * E + e_idx: tt * E + e_idx + 1]:
                e.scalar_tensor_tensor(o, i, dw, o, ALU.mult, ALU.add),
                wait=[("pe", mm_stop), ("dve", dw_tgt[tt]), ("dve", y_final[tt])],
                inc="dve")
        y_final[tt] = max(y_final[tt], wy_war[sl])

    w2loads(0, 0)
    for u in range(NU):
        for ic in range(IC):
            stage1_chunk(u, ic)
        sl = 0
        for tt in range(TT):
            for oh in range(NO2):
                stage2_chunk(u, tt, oh, sl % 2)
                sl += 1
        ht_war[u] = stage2_last_mm
        if u + 1 < NU:
            w2loads(u + 1, stage2_last_mm)

    for tt in range(TT):
        p.dma("sync", out_t[tt * 128:(tt + 1) * 128, :], y1[:, tt * D:(tt + 1) * D],
              inc="dout", wait=[("dve", y_final[tt])])
    p.emit("sync", lambda e: e.nop(), wait=[("dout", p.cnt["dout"])])

    st_ps7.close()
    st_P7.close()
    st_y1.close()

    with contextlib.ExitStack() as rs:
        sems = {}
        for name in p.cnt.keys():
            sems[name] = rs.enter_context(nc.semaphore("sem_" + name))
        block = rs.enter_context(nc.Block())
        p.replay(block, sems)
    st_persist.close()
    return nc


def _prep_inputs(cfg, x, wq, wk, wv, wo, gate_w, w1, w2, w3, sw1, sw2, sw3):
    T, D, HD, E, I, SI, R = (cfg[k] for k in ("T", "D", "HD", "E", "I", "SI", "R"))
    H, DT, IC, SH = D // HD, D // 128, I // 128, SI // I
    NU = SH + E

    def t4(w):
        wt = np.ascontiguousarray(np.asarray(w).T)
        v = wt.reshape(DT, 128, DT, 128)
        return np.ascontiguousarray(v.transpose(2, 1, 0, 3).reshape(DT, 128, D))

    def tile1(wT):
        v = wT.reshape(DT, 128, IC, 128)
        return np.ascontiguousarray(v.transpose(2, 1, 0, 3).reshape(IC, 128, DT * 128))

    def tile2(w2T):
        return np.ascontiguousarray(np.asarray(w2T).reshape(IC, 128, D))

    w1u = np.empty((NU, IC, 128, DT * 128), dtype=BF)
    w3u = np.empty((NU, IC, 128, DT * 128), dtype=BF)
    w2u = np.empty((NU, IC, 128, D), dtype=BF)
    sw1 = np.asarray(sw1); sw2 = np.asarray(sw2); sw3 = np.asarray(sw3)
    for sh in range(SH):
        sl = slice(sh * I, (sh + 1) * I)
        w1u[sh] = tile1(np.ascontiguousarray(sw1[sl, :].T))
        w3u[sh] = tile1(np.ascontiguousarray(sw3[sl, :].T))
        w2u[sh] = tile2(np.ascontiguousarray(sw2[:, sl].T))
    for e in range(E):
        w1u[SH + e] = tile1(np.ascontiguousarray(np.asarray(w1)[e].T))
        w3u[SH + e] = tile1(np.ascontiguousarray(np.asarray(w3)[e].T))
        w2u[SH + e] = tile2(np.ascontiguousarray(np.asarray(w2)[e].T))

    gT = np.ascontiguousarray(np.asarray(gate_w).T)
    gate4 = np.ascontiguousarray(gT.reshape(DT, 128, E).transpose(1, 0, 2)
                                 .reshape(128, DT * E))

    ident = np.eye(128, dtype=BF)
    permM = np.zeros((128, 128), dtype=BF)
    for i in range(0, 128, 2):
        permM[i + 1, i] = 1.0
        permM[i, i + 1] = 1.0
    onehot = np.zeros((128, H * H), dtype=BF)
    for h in range(H):
        onehot[:, h * H + h] = 1.0
    ones1 = np.ones((1, 128), dtype=np.float32)

    scale = np.arange(0, HD, 2, dtype=np.float32) / HD
    inv_freq = (1.0 / 10000.0 ** scale).astype(np.float64)

    x = np.asarray(x).reshape(-1, D)
    S_total = x.shape[0]
    assert S_total == R * T
    in_maps = []
    common = dict(
        wq4=t4(wq).astype(np.float32), wk4=t4(wk).astype(np.float32),
        wv4=t4(wv).astype(np.float32),
        wot=np.ascontiguousarray(np.asarray(wo).T).astype(np.float32),
        gate4=gate4.astype(np.float32),
        w1t=w1u, w3t=w3u, w2t=w2u,
        ident=ident, perm=permM, onehot=onehot, ones1=ones1,
        cvals=np.tile(np.array(_CONST_VALS(cfg) + [0.0] * (16 - len(_CONST_VALS(cfg))),
                              dtype=np.float32), (128, 1)),
    )
    for c in range(R):
        pos = np.arange(c * T, (c + 1) * T, dtype=np.float64)
        ang = pos[None, :] * inv_freq[:, None]
        c2 = np.empty((128, T), dtype=np.float32)
        s2 = np.empty((128, T), dtype=np.float32)
        c2[0::2, :] = np.cos(ang); c2[1::2, :] = np.cos(ang)
        s2[0::2, :] = -np.sin(ang); s2[1::2, :] = np.sin(ang)
        m = dict(common)
        m["x"] = np.ascontiguousarray(x[c * T:(c + 1) * T, :]).astype(np.float32)
        m["c2t"] = c2
        m["s2t"] = s2
        in_maps.append(m)
    return in_maps


def kernel(L0_attn_x, L0_attn_wq, L0_attn_wk, L0_attn_wv, L0_attn_wo,
           L0_ffn_gate_w, L0_ffn_w1, L0_ffn_w2, L0_ffn_w3,
           L0_ffn_sw1, L0_ffn_sw2, L0_ffn_sw3):
    cfg = CFG_FULL
    in_maps = _prep_inputs(cfg, L0_attn_x, L0_attn_wq, L0_attn_wk, L0_attn_wv,
                           L0_attn_wo, L0_ffn_gate_w, L0_ffn_w1, L0_ffn_w2,
                           L0_ffn_w3, L0_ffn_sw1, L0_ffn_sw2, L0_ffn_sw3)
    nc = build_program(cfg)
    res = run_bass_kernel_spmd(nc, in_maps, core_ids=list(range(cfg["R"])))
    out = np.concatenate([np.asarray(r["out"]) for r in res.results], axis=0)
    return out.reshape(1, cfg["R"] * cfg["T"], cfg["D"]).astype(np.float32)


# version 09
# speedup vs baseline: 1.0255x; 1.0255x over previous
import math
import numpy as np
import ml_dtypes
import concourse.bass as bass
import concourse.mybir as mybir
from concourse.bass_utils import run_bass_kernel_spmd

F32 = mybir.dt.float32
BF16 = mybir.dt.bfloat16
AF = mybir.ActivationFunctionType
ALU = mybir.AluOpType
AX = mybir.AxisListType
BF = ml_dtypes.bfloat16

CFG_FULL = dict(T=1024, D=2048, HD=128, E=16, I=1408, SI=2816, R=8)


def _CONST_VALS(cfg):
    return [1e-5, -0.5, 1.0 / math.sqrt(cfg["HD"]), 1.0, -1.0,
            1.0 / cfg["D"], 1e30, 0.0, 2.0, 0.5]


class Prog:
    ENGINES = ("tensor", "vector", "scalar", "gpsimd", "sync")

    def __init__(self, nc):
        self.nc = nc
        self.ops = {e: [] for e in self.ENGINES}
        self.cnt = {}
        self.bar_floor = 0

    def sem(self, *names):
        for n in names:
            self.cnt[n] = 0

    def emit(self, eng, fn, wait=(), inc=None, amt=1, nobar=False):
        w = [(s, v) for (s, v) in wait if v and v > 0]
        for s, v in w:
            assert v <= self.cnt[s], f"forward wait {s}:{v}>{self.cnt[s]} on {eng}"
        if not nobar and self.bar_floor:
            w.append(("bar", self.bar_floor))
        tgt = None
        if inc is not None:
            self.cnt[inc] += amt
            tgt = self.cnt[inc]
        self.ops[eng].append((fn, w, inc, amt))
        return tgt

    def dma(self, eng, out, in_, inc, wait=()):
        def fn(e, o=out, i=in_):
            return e.dma_start(out=o, in_=i)
        return self.emit(eng, fn, wait=wait, inc=inc, amt=16)

    def barrier(self):
        prev = self.bar_floor
        for e in self.ENGINES:
            if e == "sync":
                fn = lambda eng: eng.nop()
            else:
                fn = lambda eng: eng.drain()
            self.emit(e, fn, wait=([("bar", prev)] if prev else []), inc="bar",
                      nobar=True)
        self.bar_floor = self.cnt["bar"]

    def replay(self, block, sems):
        for name in self.ENGINES:
            ops = self.ops[name]

            def mk(ops=ops):
                def body(eng):
                    last = {}
                    for fn, waits, inc, amt in ops:
                        for s, v in waits:
                            if last.get(s, 0) < v:
                                eng.wait_ge(sems[s], v)
                                last[s] = v
                        ins = fn(eng)
                        if inc is not None:
                            ins.then_inc(sems[inc], amt)
                return body

            getattr(block, name)(mk())


def _treeplan(n):
    steps, cur, tmp = [], [("e", i) for i in range(n)], 0
    while len(cur) > 1:
        nxt = []
        for i in range(0, len(cur) - 1, 2):
            dst = ("s", 0) if len(cur) == 2 else ("t", tmp)
            steps.append((dst, cur[i], cur[i + 1]))
            nxt.append(dst)
            tmp += 1
        if len(cur) % 2:
            nxt.append(cur[-1])
        cur = nxt
    return steps


def build_program(cfg, stop_after=None):
    T, D, HD, E, I, SI, R = (cfg[k] for k in ("T", "D", "HD", "E", "I", "SI", "R"))
    H = D // HD
    TT = T // 128
    DT = D // 128
    IC = I // 128
    KT = T // 128
    SH = SI // I
    NU = SH + E
    Q5 = min(512, T)
    NQ = T // Q5
    RT = 1.0 / math.sqrt(HD)
    assert SI == SH * I and H * HD == D

    nc = bass.Bass(trn_type="TRN2", target_bir_lowering=False)

    x_in = nc.dram_tensor("x", [T, D], F32, kind="ExternalInput")
    c2d = nc.dram_tensor("c2t", [128, T], F32, kind="ExternalInput")
    s2d = nc.dram_tensor("s2t", [128, T], F32, kind="ExternalInput")
    wq4 = nc.dram_tensor("wq4", [DT, 128, D], F32, kind="ExternalInput")
    wk4 = nc.dram_tensor("wk4", [DT, 128, D], F32, kind="ExternalInput")
    wv4 = nc.dram_tensor("wv4", [DT, 128, D], F32, kind="ExternalInput")
    wotd = nc.dram_tensor("wot", [D, D], F32, kind="ExternalInput")
    gate4 = nc.dram_tensor("gate4", [128, DT * E], F32, kind="ExternalInput")
    w1d = nc.dram_tensor("w1t", [NU, IC, 128, DT * 128], BF16, kind="ExternalInput")
    w3d = nc.dram_tensor("w3t", [NU, IC, 128, DT * 128], BF16, kind="ExternalInput")
    w2d = nc.dram_tensor("w2t", [NU, IC, 128, D], BF16, kind="ExternalInput")
    identd = nc.dram_tensor("ident", [128, 128], BF16, kind="ExternalInput")
    permd = nc.dram_tensor("perm", [128, 128], BF16, kind="ExternalInput")
    onehotd = nc.dram_tensor("onehot", [128, H * H], BF16, kind="ExternalInput")
    ones1d = nc.dram_tensor("ones1", [1, 128], F32, kind="ExternalInput")
    cvalsd = nc.dram_tensor("cvals", [128, 16], F32, kind="ExternalInput")
    out_t = nc.dram_tensor("out", [T, D], F32, kind="ExternalOutput")
    kt_in = nc.dram_tensor("kt_in", [D, T], BF16)
    v_in = nc.dram_tensor("v_in", [T, D], BF16)
    aspace = "Shared" if R > 4 else "Local"
    kt_all = nc.dram_tensor("kt_all", [R * D, T], BF16, addr_space=aspace)
    v_all = nc.dram_tensor("v_all", [R * T, D], BF16, addr_space=aspace)

    p = Prog(nc)
    p.sem("bar", "pe", "act", "dve", "pool",
          "dc", "dx0", "dx1", "dxy", "gw0", "gw1", "gw2", "gwf",
          "dk0", "dk1", "dk2", "dk3", "dv0", "dv1",
          "dko0", "dko1", "dvs0", "dvs1",
          "da0", "da1", "db0", "db1", "dw2",
          "dfl", "dout", "cc1", "cc2")

    import contextlib
    st_persist = contextlib.ExitStack()
    SP_ = lambda stk: (lambda *a, **k: stk.enter_context(nc.sbuf_tensor(*a, **k)))
    PP_ = lambda stk: (lambda *a, **k: stk.enter_context(nc.psum_tensor(*a, **k)))
    S0 = SP_(st_persist)

    ident = S0("c_ident", [128, 128], BF16)
    perm = S0("c_perm", [128, 128], BF16)
    onehot = S0("c_onehot", [128, H * H], BF16)
    ones1 = S0("c_ones1", [1, 128], F32)
    c2 = S0("c_c2", [128, T], F32)
    s2 = S0("c_s2", [128, T], F32)
    gate = S0("c_gate", [128, DT * E], BF16)
    stats = S0("stats", [128, 16], F32)
    sgate = S0("sgate", [128, TT * E], F32)
    dwall = S0("dwall", [128, TT * E], F32)
    tke = S0("tke", [128, 5 * E], F32)
    cvals = S0("c_cvals", [128, 16], F32)

    for dst, src in ((ident, identd), (perm, permd), (onehot, onehotd),
                     (c2, c2d), (s2, s2d)):
        p.dma("sync", dst[:, :], src[:, :], inc="dc")
    p.dma("sync", ones1[0:1, :], ones1d[0:1, :], inc="dc")
    p.dma("sync", cvals[:, :], cvalsd[:, :], inc="dc")
    CONST = p.cnt["dc"]
    for ci, cv in enumerate(_CONST_VALS(cfg)):
        nc.const_aps.register(F32, cv, cvals[:, ci:ci + 1])
    for _e in Prog.ENGINES:
        p.emit(_e, lambda e: e.nop(), wait=[("dc", CONST)])
    p.dma("gpsimd", gate[:, :], gate4[:, :], inc="gwf")

    st_right = contextlib.ExitStack()
    qt = st_right.enter_context(nc.sbuf_tensor("qt", [128, H * T], BF16, side="right"))
    at = st_right.enter_context(nc.sbuf_tensor("at", [128, H * T], BF16, side="right"))

    def emit_ln(src_fn, wait_fn, done_fn, lnb, sqs, dst_t, tpsq):
        lnb_war = [0, 0]
        tps_war = [0, 0, 0]
        stats_war = [0, 0]
        ready = {}
        for tt in range(TT):
            sl = tt % 2
            so = sl * 8
            xw = list(wait_fn(tt))
            src = src_fn(tt)
            v1 = p.emit("vector", lambda e, s=src, o=stats[:, so:so + 1]:
                        e.reduce_sum(o, s, axis=AX.X),
                        wait=xw + [("act", stats_war[sl])], inc="dve")
            a2 = p.emit("scalar", lambda e, s=src, o=sqs[:, :], a=stats[:, so + 1:so + 2]:
                        e.activation(o, s, AF.Square, accum_out=a), wait=xw, inc="act")
            v = p.emit("vector", lambda e, i=stats[:, so:so + 1], o=stats[:, so + 2:so + 3]:
                       e.tensor_scalar_mul(o, i, 1.0 / D),
                       wait=[("act", a2), ("dve", v1)], inc="dve")
            v = p.emit("vector", lambda e, i=stats[:, so + 1:so + 2], o=stats[:, so + 3:so + 4]:
                       e.tensor_scalar_mul(o, i, 1.0 / D), wait=[("dve", v)], inc="dve")
            v = p.emit("vector", lambda e, m=stats[:, so + 2:so + 3], o=stats[:, so + 4:so + 5]:
                       e.tensor_mul(o, m, m), wait=[("dve", v)], inc="dve")
            vv = p.emit("vector", lambda e, a=stats[:, so + 3:so + 4],
                        b=stats[:, so + 4:so + 5], o=stats[:, so + 5:so + 6]:
                        e.tensor_sub(o, a, b), wait=[("dve", v)], inc="dve")
            al = p.emit("scalar", lambda e, i=stats[:, so + 5:so + 6], o=stats[:, so + 6:so + 7]:
                        e.activation(o, i, AF.Ln, bias=1e-5), wait=[("dve", vv)], inc="act")
            ar = p.emit("scalar", lambda e, i=stats[:, so + 6:so + 7], o=stats[:, so + 7:so + 8]:
                        e.activation(o, i, AF.Exp, scale=-0.5), wait=[("act", al)], inc="act")
            v = p.emit("vector", lambda e, m=stats[:, so + 2:so + 3], r=stats[:, so + 7:so + 8],
                       o=stats[:, so + 4:so + 5]:
                       e.tensor_mul(o, m, r), wait=[("act", ar), ("dve", vv)], inc="dve")
            vnb = p.emit("vector", lambda e, i=stats[:, so + 4:so + 5], o=stats[:, so + 4:so + 5]:
                         e.tensor_scalar_mul(o, i, -1.0), wait=[("dve", v)], inc="dve")
            alnb = p.emit("scalar",
                          lambda e, s=src, o=lnb[:, sl * D:(sl + 1) * D],
                          sc=stats[:, so + 7:so + 8], b=stats[:, so + 4:so + 5]:
                          e.activation(o, s, AF.Identity, scale=sc, bias=b),
                          wait=[("dve", vnb), ("act", ar), ("pe", lnb_war[sl])], inc="act")
            stats_war[sl] = alnb
            done_fn(tt, [("act", alnb), ("dve", v1)])
            for dt in range(DT):
                j = (tt * DT + dt) % 3
                ttr = p.emit("tensor",
                             lambda e, i=lnb[:, sl * D + dt * 128: sl * D + (dt + 1) * 128],
                             o=tpsq[j][:, :]:
                             e.transpose(o, i, ident[:, :]),
                             wait=[("act", alnb), ("dve", tps_war[j]), ("dc", CONST)],
                             inc="pe")
                if dt == DT - 1:
                    lnb_war[sl] = ttr
                tps_war[j] = p.emit("vector",
                                    lambda e, i=tpsq[j][:, :],
                                    o=dst_t[:, dt * T + tt * 128: dt * T + (tt + 1) * 128]:
                                    e.tensor_copy(o, i), wait=[("pe", ttr)], inc="dve")
                ready[(dt, tt)] = tps_war[j]
        return ready

    st_A = contextlib.ExitStack()
    SA = SP_(st_A)
    ln1t = SA("ln1t", [128, DT * T], BF16)
    xb = SA("xb", [128, 2 * D], F32)
    lnb1 = SA("lnb1", [128, 2 * D], BF16)
    sqs1 = SA("sqs1", [128, D], BF16)
    st_ps0 = contextlib.ExitStack()
    tps0 = [PP_(st_ps0)("tps0a", [128, 128], BF16),
            PP_(st_ps0)("tps0b", [128, 128], BF16),
            PP_(st_ps0)("tps0c", [128, 128], BF16)]

    x_tgt = {}
    x_tgt[0] = p.dma("sync", xb[:, 0:D], x_in[0:128, :], inc="dx0")
    if TT > 1:
        x_tgt[1] = p.dma("sync", xb[:, D:2 * D], x_in[128:256, :], inc="dx1")

    def x_done(tt, tgts):
        if tt + 2 < TT:
            x_tgt[tt + 2] = p.dma("sync", xb[:, (tt % 2) * D:(tt % 2 + 1) * D],
                                  x_in[(tt + 2) * 128:(tt + 3) * 128, :],
                                  inc="dx%d" % (tt % 2), wait=tgts)

    ln1_ready = emit_ln(lambda tt: xb[:, (tt % 2) * D:(tt % 2 + 1) * D],
                        lambda tt: [("dx%d" % (tt % 2), x_tgt[tt])],
                        x_done, lnb1, sqs1, ln1t, tps0)
    LN1_ALL = max(ln1_ready.values())
    p.barrier()
    st_ps0.close()

    def early_out(src_sb):
        p.barrier()
        eng = "sync" if src_sb.tensor.dtype == F32 else "gpsimd"
        for tt in range(TT):
            p.dma(eng, out_t[tt * 128:(tt + 1) * 128, :], src_sb,
                  inc="dout", wait=[])
        p.emit("sync", lambda e: e.nop(), wait=[("dout", p.cnt["dout"])])
        with contextlib.ExitStack() as rs:
            sems = {}
            for name in p.cnt.keys():
                sems[name] = rs.enter_context(nc.semaphore("sem_" + name))
            block = rs.enter_context(nc.Block())
            p.replay(block, sems)
        return nc

    if stop_after == "p0":
        return early_out(ln1t[:, 0:D])

    st_P1 = contextlib.ExitStack()
    S1 = SP_(st_P1)
    wb = S1("wb", [128, 3 * D], BF16)
    qe = S1("qe", [128, 3 * T], BF16)
    t1b = S1("t1b", [128, T], F32)
    t2b = S1("t2b", [128, T], F32)
    kst = S1("kst", [128, 2 * T], BF16)
    vst = S1("vst", [128, 2 * T], BF16)
    st_ps1 = contextlib.ExitStack()
    pps = PP_(st_ps1)("pps", [128, 2 * T], F32)
    st_ps1qk = contextlib.ExitStack()
    sps = PP_(st_ps1qk)("sps", [128, 2 * T], F32)

    wb_war = [0, 0, 0]
    pps_war = [[], []]
    sps_war = [[], []]
    qe_war = [[], [], []]
    t12_war = [0]
    kst_war = [0, 0]
    vst_war = [0, 0]
    vtb_war = [0, 0, 0]

    def proj(kind, w4, limit=None):
        for ot in range(DT):
            sl = ot % 2
            sl3 = ot % 3
            ld = p.dma("gpsimd", wb[:, sl3 * D:(sl3 + 1) * D], w4[ot, :, :],
                       inc="gw%d" % sl3, wait=[("pe", wb_war[sl3])])
            mm_stop = None
            for qn in range(NQ):
                for dt in range(DT):
                    w = [("gw%d" % sl3, ld), ("dve", LN1_ALL)]
                    if dt == 0 and qn == 0:
                        w += pps_war[sl]
                    mm_stop = p.emit(
                        "tensor",
                        lambda e, o=pps[:, sl * T + qn * Q5: sl * T + qn * Q5 + Q5],
                        l=wb[:, sl3 * D + dt * 128: sl3 * D + (dt + 1) * 128],
                        r=ln1t[:, dt * T + qn * Q5: dt * T + qn * Q5 + Q5],
                        s0=(dt == 0), s1=(dt == DT - 1):
                        e.matmul(o, l, r, start=s0, stop=s1),
                        wait=w, inc="pe")
            wb_war[sl3] = mm_stop
            pslice = pps[:, sl * T:(sl + 1) * T]
            if kind in ("q", "k"):
                aev = p.emit("scalar", lambda e, i=pslice, o=qe[:, sl3 * T:(sl3 + 1) * T]:
                             e.activation(o, i, AF.Copy),
                             wait=[("pe", mm_stop)] + qe_war[sl3], inc="act")
                if limit == "mm":
                    pps_war[sl] = [("act", aev)]
                    qe_war[sl3] = [("pe", mm_stop)]
                    continue
                sw_stop = None
                for qn in range(NQ):
                    w = [("act", aev)]
                    if qn == 0:
                        w += sps_war[sl]
                    sw_stop = p.emit(
                        "tensor",
                        lambda e, o=sps[:, sl * T + qn * Q5: sl * T + qn * Q5 + Q5],
                        r=qe[:, sl3 * T + qn * Q5: sl3 * T + qn * Q5 + Q5]:
                        e.matmul(o, perm[:, :], r, start=True, stop=True),
                        wait=w, inc="pe")
                qe_war[sl3] = [("pe", sw_stop)]
                if limit == "swap":
                    pps_war[sl] = [("act", aev)]
                    sps_war[sl] = []
                    continue
                v1 = p.emit("vector", lambda e, i=qe[:, sl3 * T:(sl3 + 1) * T]:
                            e.tensor_mul(t1b[:, :], i, c2[:, :]),
                            wait=[("act", aev), ("dve", t12_war[0])], inc="dve")
                v2 = p.emit("vector", lambda e, i=sps[:, sl * T:(sl + 1) * T]:
                            e.tensor_mul(t2b[:, :], i, s2[:, :]),
                            wait=[("pe", sw_stop), ("dve", t12_war[0])], inc="dve")
                pps_war[sl] = [("act", aev)]
                sps_war[sl] = [("dve", v2)]
                qe_war[sl3] = [("pe", sw_stop), ("dve", v1)]
                if limit == "rope1":
                    t12_war[0] = v2
                    continue
                if kind == "q":
                    dst = qt[:, ot * T:(ot + 1) * T]
                    vo = p.emit("vector", lambda e, d=dst:
                                e.tensor_add(d, t1b[:, :], t2b[:, :]),
                                wait=[("dve", v1), ("dve", v2)], inc="dve")
                    t12_war[0] = vo
                else:
                    dst = kst[:, sl * T:(sl + 1) * T]
                    vo = p.emit("vector", lambda e, d=dst:
                                e.tensor_add(d, t1b[:, :], t2b[:, :]),
                                wait=[("dve", v1), ("dve", v2),
                                      ("dko%d" % sl, kst_war[sl])], inc="dve")
                    t12_war[0] = vo
                    kst_war[sl] = p.dma("sync", kt_in[ot * 128:(ot + 1) * 128, :],
                                        dst, inc="dko%d" % sl, wait=[("dve", vo)])
            else:
                aev = p.emit("scalar", lambda e, i=pslice, o=qe[:, sl3 * T:(sl3 + 1) * T]:
                             e.activation(o, i, AF.Copy),
                             wait=[("pe", mm_stop)] + qe_war[sl3], inc="act")
                pps_war[sl] = [("act", aev)]
                last_cp = None
                for tc in range(TT):
                    j = (ot * TT + tc) % 3
                    ttr = p.emit("tensor",
                                 lambda e, i=qe[:, sl3 * T + tc * 128: sl3 * T + (tc + 1) * 128],
                                 o=vtp[j][:, :]:
                                 e.transpose(o, i, ident[:, :]),
                                 wait=[("act", aev), ("dve", vtb_war[j])], inc="pe")
                    if tc == TT - 1:
                        qe_war[sl3] = [("pe", ttr)]
                    w = [("pe", ttr)]
                    if tc == 0:
                        w.append(("dvs%d" % sl, vst_war[sl]))
                    last_cp = p.emit("vector",
                                     lambda e, i=vtp[j][:, :],
                                     o=vst[:, sl * T + tc * 128: sl * T + (tc + 1) * 128]:
                                     e.tensor_copy(o, i), wait=w, inc="dve")
                    vtb_war[j] = last_cp
                dst_ap = v_in[:, ot * 128:(ot + 1) * 128].rearrange(
                    "(tc p) hd -> p tc hd", p=128)
                src_ap = vst[:, sl * T:(sl + 1) * T].rearrange(
                    "p (tc hd) -> p tc hd", hd=128)
                vst_war[sl] = p.dma("sync", dst_ap, src_ap,
                                    inc="dvs%d" % sl, wait=[("dve", last_cp)])

    if stop_after in ("p1q_mm", "p1q_swap", "p1q_rope1"):
        proj("q", wq4, limit=stop_after[4:])
        return early_out(qe[:, 0:D])
    proj("q", wq4)
    if stop_after == "p1q":
        return early_out(qt[:, 0:D])
    proj("k", wk4)
    k_wait = [(s, p.cnt[s]) for s in ("dko0", "dko1")]
    cc1t = p.emit("gpsimd", lambda e: e.collective_compute(
        "AllGather", ALU.bypass, replica_groups=[list(range(R))],
        ins=[kt_in.ap().opt()], outs=[kt_all.ap().opt()]),
        wait=k_wait, inc="cc1", amt=1)
    p.barrier()
    st_ps1qk.close()
    st_ps1v = contextlib.ExitStack()
    vtp = [PP_(st_ps1v)("vtpa", [128, 128], BF16),
           PP_(st_ps1v)("vtpb", [128, 128], BF16),
           PP_(st_ps1v)("vtpc", [128, 128], BF16)]
    proj("v", wv4)
    if stop_after == "p1":
        return early_out(qt[:, 0:D])
    v_wait = [(s, p.cnt[s]) for s in ("dvs0", "dvs1")]
    cc2t = p.emit("gpsimd", lambda e: e.collective_compute(
        "AllGather", ALU.bypass, replica_groups=[list(range(R))],
        ins=[v_in.ap().opt()], outs=[v_all.ap().opt()]),
        wait=v_wait, inc="cc2", amt=1)

    if stop_after == "cc":
        return early_out(qt[:, 0:D])
    p.barrier()
    st_ps1v.close()
    st_ps1.close()
    st_P1.close()
    st_A.close()

    st_P3 = contextlib.ExitStack()
    S3 = SP_(st_P3)
    ktb = S3("ktb", [128, 4 * T], BF16)
    vb = S3("vb", [128, 2 * KT * D], BF16)
    expt = S3("expt", [128, 2 * KT * T], BF16)
    tplan = _treeplan(KT)
    ntmp = sum(1 for d, _, _ in tplan if d[0] == "t")
    tbuf = S3("tbuf", [128, max(1, ntmp) * T], BF16)
    sch = S3("sch", [128, 2 * T], BF16)
    st_ps3 = contextlib.ExitStack()
    scp = PP_(st_ps3)("scp", [128, 2 * T], F32)
    pvp = PP_(st_ps3)("pvp", [128, T], F32)
    psS = PP_(st_ps3)("psS", [H, T], F32)

    ktb_war = [0, 0, 0, 0]
    vb_war = [0, 0]
    kt_tgt = {}
    v_tgt = {}

    def load_kt(m):
        r, h = divmod(m, H)
        kt_tgt[m] = p.dma("sync", ktb[:, (m % 4) * T:(m % 4 + 1) * T],
                          kt_all[r * D + h * 128: r * D + (h + 1) * 128, :],
                          inc="dk%d" % (m % 4),
                          wait=[("pe", ktb_war[m % 4]), ("cc1", 1)])

    def load_v(r):
        src = v_all[r * T:(r + 1) * T, :].rearrange("(kt p) d -> p kt d", p=128)
        dst = vb[:, (r % 2) * KT * D:(r % 2 + 1) * KT * D].rearrange(
            "p (kt d) -> p kt d", d=D)
        v_tgt[r] = p.dma("sync", dst, src,
                         inc="dv%d" % (r % 2),
                         wait=[("pe", vb_war[r % 2]), ("cc2", 1)])

    load_v(0)
    for m0 in range(min(4, R * H)):
        load_kt(m0)

    scp_war = [0, 0]
    pvp_war = []
    expt_war = [[], []]
    sch_war = [0, 0]
    smm_last = 0
    at_last = [0] * H
    pend_dsum = None
    ndsum = [0]

    def emit_dsum(mm_, hh, wlist):
        nonlocal smm_last
        for qn in range(NQ):
            smm_last = p.emit(
                "tensor",
                lambda e, o=psS[0:H, qn * Q5:(qn + 1) * Q5],
                l=onehot[:, hh * H:(hh + 1) * H],
                r_=sch[:, (mm_ % 2) * T + qn * Q5:(mm_ % 2) * T + qn * Q5 + Q5],
                s0=(ndsum[0] == 0), s1=(ndsum[0] == R * H - 1):
                e.matmul(o, l, r_, start=s0, stop=s1),
                wait=wlist, inc="pe")
        ndsum[0] += 1
        sch_war[mm_ % 2] = smm_last

    for m in range(R * H):
        r, h = divmod(m, H)
        s2_ = m % 2
        if h == 0 and r + 1 < R:
            load_v(r + 1)
        ex_t = {}
        pv_stop = None

        def emit_sc(kt):
            j = kt % 2
            mm = None
            for qn in range(NQ):
                w = [("dk%d" % (m % 4), kt_tgt[m]), ("act", scp_war[j])]
                mm = p.emit("tensor",
                            lambda e, o=scp[:, j * T + qn * Q5: j * T + qn * Q5 + Q5],
                            l=ktb[:, (m % 4) * T + kt * 128:(m % 4) * T + (kt + 1) * 128],
                            r_=qt[:, h * T + qn * Q5: h * T + qn * Q5 + Q5]:
                            e.matmul(o, l, r_, start=True, stop=True),
                            wait=w, inc="pe")
            w2 = [("pe", mm)]
            if kt == 0:
                w2 += expt_war[s2_]
            ex_t[kt] = p.emit(
                "scalar",
                lambda e, i=scp[:, j * T:(j + 1) * T],
                o=expt[:, s2_ * KT * T + kt * T: s2_ * KT * T + (kt + 1) * T]:
                e.activation(o, i, AF.Exp, scale=RT), wait=w2, inc="act")
            scp_war[j] = ex_t[kt]

        def emit_pv(kt):
            nonlocal pv_stop
            for qn in range(NQ):
                w = [("act", ex_t[kt]), ("dv%d" % (r % 2), v_tgt[r])]
                if kt == 0 and qn == 0:
                    w += pvp_war
                pv_stop = p.emit(
                    "tensor",
                    lambda e, o=pvp[:, qn * Q5: qn * Q5 + Q5],
                    l=vb[:, (r % 2) * KT * D + kt * D + h * 128:
                          (r % 2) * KT * D + kt * D + (h + 1) * 128],
                    r_=expt[:, s2_ * KT * T + kt * T + qn * Q5:
                            s2_ * KT * T + kt * T + qn * Q5 + Q5],
                    s0=(kt == 0), s1=(kt == KT - 1):
                    e.matmul(o, l, r_, start=s0, stop=s1),
                    wait=w, inc="pe")

        emit_sc(0)
        if KT > 1:
            emit_sc(1)
        for kt in range(KT):
            if kt + 2 < KT:
                emit_sc(kt + 2)
            emit_pv(kt)
        if pend_dsum is not None:
            emit_dsum(*pend_dsum)
            pend_dsum = None
        ktb_war[m % 4] = pv_stop
        if h == H - 1:
            vb_war[r % 2] = pv_stop
        node_t = {}
        lastv = None
        lastp = 0
        for si, (dst, a, b) in enumerate(tplan):
            def ap_of(n):
                k, i = n
                if k == "e":
                    return expt[:, s2_ * KT * T + i * T:(s2_ * KT * T + (i + 1) * T)]
                if k == "t":
                    return tbuf[:, i * T:(i + 1) * T]
                return sch[:, (m % 2) * T:(m % 2 + 1) * T]
            eng = "vector"
            w = []
            for n in (a, b):
                if n[0] == "e":
                    w.append(("act", ex_t[n[1]]))
                elif n in node_t:
                    w.append(node_t[n])
            if dst[0] == "s":
                w.append(("pe", sch_war[m % 2]))
            tgt = p.emit(eng, lambda e, o=ap_of(dst), x=ap_of(a), y=ap_of(b):
                         e.tensor_add(o, x, y), wait=w, inc="dve")
            node_t[dst] = ("dve", tgt)
            lastv = tgt
        pend_dsum = (m, h, [node_t[("s", 0)]])
        ew = [("pe", pv_stop)]
        if lastv:
            ew.append(("dve", lastv))
        expt_war[s2_] = ew
        if r == 0:
            vac = p.emit("vector", lambda e, i=pvp[:, :],
                         o=at[:, h * T:(h + 1) * T]:
                         e.tensor_copy(o, i), wait=[("pe", pv_stop)], inc="dve")
        else:
            vac = p.emit("vector", lambda e, i=pvp[:, :],
                         o=at[:, h * T:(h + 1) * T]:
                         e.tensor_add(o, o, i),
                         wait=[("pe", pv_stop), ("dve", at_last[h])], inc="dve")
        at_last[h] = vac
        pvp_war = [("dve", vac)]
        if m + 4 < R * H:
            load_kt(m + 4)

    if stop_after == "p3":
        return early_out(at[:, 0:D])

    if pend_dsum is not None:
        emit_dsum(*pend_dsum)

    p.barrier()
    st_P3.close()
    st_P4 = contextlib.ExitStack()
    S4 = SP_(st_P4)
    sums = S4("sums", [H, T], F32)
    rsums = S4("rsums", [H, T], F32)
    rsflat = S4("rsflat", [1, H * T], F32)
    vsc = p.emit("vector", lambda e: e.tensor_copy(sums[0:H, :], psS[0:H, :]),
                 wait=[("pe", smm_last)], inc="dve")
    vrc = p.emit("vector", lambda e: e.reciprocal(rsums[0:H, :], sums[0:H, :]),
                 wait=[("dve", vsc)], inc="dve")
    p.barrier()
    st_ps3.close()
    st_ps4 = contextlib.ExitStack()
    rps = PP_(st_ps4)("rps", [128, 2 * T], F32)
    at_sc_h = {}
    fl = p.dma("sync", rsflat[0:1, :].rearrange("a (h t) -> a h t", h=H),
               rsums[0:H, :],
               inc="dfl", wait=[("dve", vrc)])
    rps_war = [0, 0]
    for h in range(H):
        r_stop = None
        for qn in range(NQ):
            w = [("dfl", fl)]
            if qn == 0:
                w.append(("dve", rps_war[h % 2]))
            r_stop = p.emit("tensor",
                            lambda e, o=rps[:, (h % 2) * T + qn * Q5:
                                            (h % 2) * T + qn * Q5 + Q5],
                            r_=rsflat[0:1, h * T + qn * Q5: h * T + qn * Q5 + Q5]:
                            e.matmul(o, ones1[0:1, :], r_, start=True, stop=True),
                            wait=w, inc="pe")
        rps_war[h % 2] = p.emit("vector",
                                lambda e, o=at[:, h * T:(h + 1) * T],
                                i=rps[:, (h % 2) * T:(h % 2 + 1) * T]:
                                e.tensor_mul(o, o, i),
                                wait=[("pe", r_stop), ("dve", at_last[h])], inc="dve")
        at_last[h] = rps_war[h % 2]
        at_sc_h[h] = rps_war[h % 2]
    AT_SCALED = p.cnt["dve"]
    p.barrier()
    st_ps4.close()
    st_P4.close()

    if stop_after == "p4":
        return early_out(at[:, 0:D])

    st_y1 = contextlib.ExitStack()
    y1 = SP_(st_y1)("y1", [128, TT * D], F32)
    for tt in range(TT):
        p.dma("scalar", y1[:, tt * D:(tt + 1) * D],
              x_in[tt * 128:(tt + 1) * 128, :], inc="dxy")
    XY_ALL = p.cnt["dxy"]
    st_P5 = contextlib.ExitStack()
    wo_sb = SP_(st_P5)("wo_sb", [128, DT * D], BF16)
    OH = 2 if D >= 1024 else 1
    OW = D // OH
    O5 = min(512, OW)
    NO5 = OW // O5
    st_ps5 = contextlib.ExitStack()
    wop = PP_(st_ps5)("wop", [128, 2 * OW], F32)

    for oh in range(OH):
        for dt in range(DT):
            p.dma("gpsimd", wo_sb[:, dt * D + oh * OW: dt * D + (oh + 1) * OW],
                  wotd[dt * 128:(dt + 1) * 128, oh * OW:(oh + 1) * OW], inc="gwf")
    WO_ALL = p.cnt["gwf"]
    y1_attn = {}
    wop_war = [0, 0]
    for tt in range(TT):
        for oh in range(OH):
            sl = (tt * OH + oh) % 2
            mm_stop = None
            for n5 in range(NO5):
                for dt in range(DT):
                    w = [("gwf", WO_ALL), ("dve", at_sc_h[dt])]
                    if dt == 0 and n5 == 0:
                        w += [("dve", wop_war[sl])]
                    mm_stop = p.emit(
                        "tensor",
                        lambda e, o=wop[:, sl * OW + n5 * O5: sl * OW + n5 * O5 + O5],
                        l=at[:, dt * T + tt * 128: dt * T + (tt + 1) * 128],
                        r_=wo_sb[:, dt * D + oh * OW + n5 * O5:
                                 dt * D + oh * OW + n5 * O5 + O5],
                        s0=(dt == 0), s1=(dt == DT - 1):
                        e.matmul(o, l, r_, start=s0, stop=s1),
                        wait=w, inc="pe")
            wop_war[sl] = p.emit(
                "vector",
                lambda e, o=y1[:, tt * D + oh * OW: tt * D + (oh + 1) * OW],
                i=wop[:, sl * OW:(sl + 1) * OW]:
                e.tensor_add(o, o, i),
                wait=[("pe", mm_stop), ("dxy", XY_ALL)], inc="dve")
        y1_attn[tt] = wop_war[(tt * OH + OH - 1) % 2]
    p.barrier()
    st_ps5.close()
    st_P5.close()
    st_right.close()

    if stop_after == "p5":
        return early_out(y1[:, 0:D])

    st_P6 = contextlib.ExitStack()
    S6 = SP_(st_P6)
    ln2t = SP_(st_y1)("ln2t", [128, DT * T], BF16)
    lnb2 = S6("lnb2", [128, 2 * D], BF16)
    sqs2 = S6("sqs2", [128, D], BF16)
    st_ps6 = contextlib.ExitStack()
    tps6 = [PP_(st_ps6)("tps6a", [128, 128], BF16),
            PP_(st_ps6)("tps6b", [128, 128], BF16),
            PP_(st_ps6)("tps6c", [128, 128], BF16)]
    gps = [PP_(st_ps6)("gpsa", [128, E], F32),
           PP_(st_ps6)("gpsb", [128, E], F32)]

    ln2_ready = emit_ln(lambda tt: y1[:, tt * D:(tt + 1) * D],
                        lambda tt: [("dve", y1_attn[tt])],
                        lambda tt, tg: None, lnb2, sqs2, ln2t, tps6)
    gps_war = [0, 0]
    dw_tgt = {}
    tke_war = [0]
    for tt in range(TT):
        sl = tt % 2
        g_stop = None
        for dt in range(DT):
            w = [("dve", ln2_ready[(dt, tt)]), ("gwf", p.cnt["gwf"])]
            if dt == 0:
                w.append(("act", gps_war[sl]))
            g_stop = p.emit("tensor",
                            lambda e, o=gps[sl][:, :],
                            l=ln2t[:, dt * T + tt * 128: dt * T + (tt + 1) * 128],
                            r_=gate[:, dt * E:(dt + 1) * E],
                            s0=(dt == 0), s1=(dt == DT - 1):
                            e.matmul(o, l, r_, start=s0, stop=s1),
                            wait=w, inc="pe")
        asg = p.emit("scalar", lambda e, i=gps[sl][:, :],
                     o=sgate[:, tt * E:(tt + 1) * E]:
                     e.activation(o, i, AF.Sigmoid),
                     wait=[("pe", g_stop)], inc="act")
        gps_war[sl] = asg
        sg = sgate[:, tt * E:(tt + 1) * E]
        v = p.emit("vector", lambda e, i=sg: e.reduce_max(tke[:, 4 * E:4 * E + 1], i, axis=AX.X),
                   wait=[("act", asg), ("dve", tke_war[0])], inc="dve")
        v = p.emit("vector", lambda e, i=sg: e.tensor_scalar(
            tke[:, 0:E], i, tke[:, 4 * E:4 * E + 1], None, ALU.is_ge),
            wait=[("dve", v)], inc="dve")
        v = p.emit("vector", lambda e: e.tensor_scalar(
            tke[:, E:2 * E], tke[:, 0:E], 1e30, None, ALU.mult),
            wait=[("dve", v)], inc="dve")
        v = p.emit("vector", lambda e, i=sg: e.tensor_sub(
            tke[:, 2 * E:3 * E], i, tke[:, E:2 * E]), wait=[("dve", v)], inc="dve")
        v = p.emit("vector", lambda e: e.reduce_max(
            tke[:, 4 * E + 1:4 * E + 2], tke[:, 2 * E:3 * E], axis=AX.X),
            wait=[("dve", v)], inc="dve")
        v = p.emit("vector", lambda e: e.tensor_scalar(
            tke[:, E:2 * E], tke[:, 2 * E:3 * E], tke[:, 4 * E + 1:4 * E + 2],
            None, ALU.is_ge), wait=[("dve", v)], inc="dve")
        v = p.emit("vector", lambda e: e.tensor_add(
            tke[:, 3 * E:4 * E], tke[:, 0:E], tke[:, E:2 * E]),
            wait=[("dve", v)], inc="dve")
        v = p.emit("vector", lambda e, i=sg: e.tensor_mul(
            tke[:, 0:E], tke[:, 3 * E:4 * E], i), wait=[("dve", v)], inc="dve")
        v = p.emit("vector", lambda e: e.tensor_add(
            tke[:, 4 * E + 2:4 * E + 3], tke[:, 4 * E:4 * E + 1],
            tke[:, 4 * E + 1:4 * E + 2]), wait=[("dve", v)], inc="dve")
        v = p.emit("vector", lambda e: e.reciprocal(
            tke[:, 4 * E + 3:4 * E + 4], tke[:, 4 * E + 2:4 * E + 3]),
            wait=[("dve", v)], inc="dve")
        dw_tgt[tt] = p.emit("vector", lambda e, o=dwall[:, tt * E:(tt + 1) * E]:
                            e.tensor_scalar(o, tke[:, 0:E], tke[:, 4 * E + 3:4 * E + 4],
                                            None, ALU.mult), wait=[("dve", v)], inc="dve")
        tke_war[0] = dw_tgt[tt]
    p.barrier()
    st_ps6.close()
    st_P6.close()

    if stop_after == "p6":
        return early_out(y1[:, 0:D])

    st_P7 = contextlib.ExitStack()
    S7 = SP_(st_P7)
    ht = S7("ht", [128, IC * T], BF16)
    s1a = S7("s1a", [128, 2 * D], BF16)
    s1b = S7("s1b", [128, 2 * D], BF16)
    w2sb = S7("w2sb", [128, IC * D], BF16)
    silu = S7("silu", [128, T], F32)
    OW2 = min(1024, D)
    NO2 = D // OW2
    st_ps7 = contextlib.ExitStack()
    aps = PP_(st_ps7)("aps", [128, T], F32)
    bps = PP_(st_ps7)("bps", [128, T], F32)
    yps = PP_(st_ps7)("yps", [128, 2 * OW2], F32)

    s1a_war = [0, 0]
    s1b_war = [0, 0]
    aps_war = 0
    bps_war = 0
    ht_war = {}
    ht_tgt = {}
    w2_tgt = {}
    wy_war = [0, 0]
    y_final = dict((tt, 0) for tt in range(TT))
    stage2_last_mm = 0

    def w2loads(u, wait_pe):
        for ic in range(IC):
            p.dma("gpsimd", w2sb[:, ic * D:(ic + 1) * D],
                  w2d[u, ic, :, :], inc="dw2",
                  wait=([("pe", wait_pe)] if ic == 0 else []))
        w2_tgt[u] = p.cnt["dw2"]

    def stage1_chunk(u, ic):
        nonlocal aps_war, bps_war
        sl = ic % 2
        la = p.dma("sync", s1a[:, sl * D:(sl + 1) * D], w1d[u, ic, :, :],
                   inc="da%d" % sl, wait=[("pe", s1a_war[sl])])
        lb = p.dma("sync", s1b[:, sl * D:(sl + 1) * D], w3d[u, ic, :, :],
                   inc="db%d" % sl, wait=[("pe", s1b_war[sl])])
        a_stop = b_stop = None
        for qn in range(NQ):
            for dt in range(DT):
                w = [("da%d" % sl, la), ("dve", ln2_ready[(dt, TT - 1)])]
                if dt == 0 and qn == 0:
                    w.append(("act", aps_war))
                a_stop = p.emit(
                    "tensor",
                    lambda e, o=aps[:, qn * Q5: qn * Q5 + Q5],
                    l=s1a[:, sl * D + dt * 128: sl * D + (dt + 1) * 128],
                    r_=ln2t[:, dt * T + qn * Q5: dt * T + qn * Q5 + Q5],
                    s0=(dt == 0), s1=(dt == DT - 1):
                    e.matmul(o, l, r_, start=s0, stop=s1), wait=w, inc="pe")
        s1a_war[sl] = a_stop
        for qn in range(NQ):
            for dt in range(DT):
                w = [("db%d" % sl, lb)]
                if dt == 0 and qn == 0:
                    w.append(("dve", bps_war))
                b_stop = p.emit(
                    "tensor",
                    lambda e, o=bps[:, qn * Q5: qn * Q5 + Q5],
                    l=s1b[:, sl * D + dt * 128: sl * D + (dt + 1) * 128],
                    r_=ln2t[:, dt * T + qn * Q5: dt * T + qn * Q5 + Q5],
                    s0=(dt == 0), s1=(dt == DT - 1):
                    e.matmul(o, l, r_, start=s0, stop=s1), wait=w, inc="pe")
        s1b_war[sl] = b_stop
        aps_war = p.emit("scalar", lambda e: e.activation(silu[:, :], aps[:, :], AF.Silu),
                         wait=[("pe", a_stop), ("dve", bps_war)], inc="act")
        bps_war = p.emit("vector", lambda e, o=ht[:, ic * T:(ic + 1) * T]:
                         e.tensor_mul(o, silu[:, :], bps[:, :]),
                         wait=[("act", aps_war), ("pe", b_stop),
                               ("pe", ht_war.get(u - 1, 0))], inc="dve")
        ht_tgt[(u, ic)] = bps_war

    def stage2_chunk(u, tt, oh, sl):
        nonlocal stage2_last_mm
        e_idx = u - SH
        W5 = min(Q5, OW2)
        N5 = OW2 // W5
        mm_stop = None
        for n5 in range(N5):
            for ic in range(IC):
                w = [("dve", ht_tgt[(u, ic)]), ("dw2", w2_tgt[u])]
                if ic == 0 and n5 == 0:
                    w.append(("dve", wy_war[sl]))
                mm_stop = p.emit(
                    "tensor",
                    lambda e, o=yps[:, sl * OW2 + n5 * W5: sl * OW2 + n5 * W5 + W5],
                    l=ht[:, ic * T + tt * 128: ic * T + (tt + 1) * 128],
                    r_=w2sb[:, ic * D + oh * OW2 + n5 * W5:
                            ic * D + oh * OW2 + n5 * W5 + W5],
                    s0=(ic == 0), s1=(ic == IC - 1):
                    e.matmul(o, l, r_, start=s0, stop=s1), wait=w, inc="pe")
        stage2_last_mm = mm_stop
        ysl = y1[:, tt * D + oh * OW2: tt * D + (oh + 1) * OW2]
        if e_idx < 0:
            wy_war[sl] = p.emit("vector",
                                lambda e, o=ysl, i=yps[:, sl * OW2:(sl + 1) * OW2]:
                                e.tensor_add(o, o, i),
                                wait=[("pe", mm_stop), ("dve", y_final[tt])], inc="dve")
        else:
            wy_war[sl] = p.emit(
                "vector",
                lambda e, o=ysl, i=yps[:, sl * OW2:(sl + 1) * OW2],
                dw=dwall[:, tt * E + e_idx: tt * E + e_idx + 1]:
                e.scalar_tensor_tensor(o, i, dw, o, ALU.mult, ALU.add),
                wait=[("pe", mm_stop), ("dve", dw_tgt[tt]), ("dve", y_final[tt])],
                inc="dve")
        y_final[tt] = max(y_final[tt], wy_war[sl])

    w2loads(0, 0)
    for u in range(NU):
        for ic in range(IC):
            stage1_chunk(u, ic)
        sl = 0
        for tt in range(TT):
            for oh in range(NO2):
                stage2_chunk(u, tt, oh, sl % 2)
                sl += 1
        ht_war[u] = stage2_last_mm
        if u + 1 < NU:
            w2loads(u + 1, stage2_last_mm)

    for tt in range(TT):
        p.dma("sync", out_t[tt * 128:(tt + 1) * 128, :], y1[:, tt * D:(tt + 1) * D],
              inc="dout", wait=[("dve", y_final[tt])])
    p.emit("sync", lambda e: e.nop(), wait=[("dout", p.cnt["dout"])])

    st_ps7.close()
    st_P7.close()
    st_y1.close()

    with contextlib.ExitStack() as rs:
        sems = {}
        for name in p.cnt.keys():
            sems[name] = rs.enter_context(nc.semaphore("sem_" + name))
        block = rs.enter_context(nc.Block())
        p.replay(block, sems)
    st_persist.close()
    return nc


def _prep_inputs(cfg, x, wq, wk, wv, wo, gate_w, w1, w2, w3, sw1, sw2, sw3):
    T, D, HD, E, I, SI, R = (cfg[k] for k in ("T", "D", "HD", "E", "I", "SI", "R"))
    H, DT, IC, SH = D // HD, D // 128, I // 128, SI // I
    NU = SH + E

    def t4(w):
        wt = np.ascontiguousarray(np.asarray(w).T)
        v = wt.reshape(DT, 128, DT, 128)
        return np.ascontiguousarray(v.transpose(2, 1, 0, 3).reshape(DT, 128, D))

    def tile1(wT):
        v = wT.reshape(DT, 128, IC, 128)
        return np.ascontiguousarray(v.transpose(2, 1, 0, 3).reshape(IC, 128, DT * 128))

    def tile2(w2T):
        return np.ascontiguousarray(np.asarray(w2T).reshape(IC, 128, D))

    w1u = np.empty((NU, IC, 128, DT * 128), dtype=BF)
    w3u = np.empty((NU, IC, 128, DT * 128), dtype=BF)
    w2u = np.empty((NU, IC, 128, D), dtype=BF)
    sw1 = np.asarray(sw1); sw2 = np.asarray(sw2); sw3 = np.asarray(sw3)
    for sh in range(SH):
        sl = slice(sh * I, (sh + 1) * I)
        w1u[sh] = tile1(np.ascontiguousarray(sw1[sl, :].T))
        w3u[sh] = tile1(np.ascontiguousarray(sw3[sl, :].T))
        w2u[sh] = tile2(np.ascontiguousarray(sw2[:, sl].T))
    for e in range(E):
        w1u[SH + e] = tile1(np.ascontiguousarray(np.asarray(w1)[e].T))
        w3u[SH + e] = tile1(np.ascontiguousarray(np.asarray(w3)[e].T))
        w2u[SH + e] = tile2(np.ascontiguousarray(np.asarray(w2)[e].T))

    gT = np.ascontiguousarray(np.asarray(gate_w).T)
    gate4 = np.ascontiguousarray(gT.reshape(DT, 128, E).transpose(1, 0, 2)
                                 .reshape(128, DT * E))

    ident = np.eye(128, dtype=BF)
    permM = np.zeros((128, 128), dtype=BF)
    for i in range(0, 128, 2):
        permM[i + 1, i] = 1.0
        permM[i, i + 1] = 1.0
    onehot = np.zeros((128, H * H), dtype=BF)
    for h in range(H):
        onehot[:, h * H + h] = 1.0
    ones1 = np.ones((1, 128), dtype=np.float32)

    scale = np.arange(0, HD, 2, dtype=np.float32) / HD
    inv_freq = (1.0 / 10000.0 ** scale).astype(np.float64)

    x = np.asarray(x).reshape(-1, D)
    S_total = x.shape[0]
    assert S_total == R * T
    in_maps = []
    common = dict(
        wq4=t4(wq).astype(np.float32), wk4=t4(wk).astype(np.float32),
        wv4=t4(wv).astype(np.float32),
        wot=np.ascontiguousarray(np.asarray(wo).T).astype(np.float32),
        gate4=gate4.astype(np.float32),
        w1t=w1u, w3t=w3u, w2t=w2u,
        ident=ident, perm=permM, onehot=onehot, ones1=ones1,
        cvals=np.tile(np.array(_CONST_VALS(cfg) + [0.0] * (16 - len(_CONST_VALS(cfg))),
                              dtype=np.float32), (128, 1)),
    )
    for c in range(R):
        pos = np.arange(c * T, (c + 1) * T, dtype=np.float64)
        ang = pos[None, :] * inv_freq[:, None]
        c2 = np.empty((128, T), dtype=np.float32)
        s2 = np.empty((128, T), dtype=np.float32)
        c2[0::2, :] = np.cos(ang); c2[1::2, :] = np.cos(ang)
        s2[0::2, :] = -np.sin(ang); s2[1::2, :] = np.sin(ang)
        m = dict(common)
        m["x"] = np.ascontiguousarray(x[c * T:(c + 1) * T, :]).astype(np.float32)
        m["c2t"] = c2
        m["s2t"] = s2
        in_maps.append(m)
    return in_maps


def kernel(L0_attn_x, L0_attn_wq, L0_attn_wk, L0_attn_wv, L0_attn_wo,
           L0_ffn_gate_w, L0_ffn_w1, L0_ffn_w2, L0_ffn_w3,
           L0_ffn_sw1, L0_ffn_sw2, L0_ffn_sw3):
    cfg = CFG_FULL
    in_maps = _prep_inputs(cfg, L0_attn_x, L0_attn_wq, L0_attn_wk, L0_attn_wv,
                           L0_attn_wo, L0_ffn_gate_w, L0_ffn_w1, L0_ffn_w2,
                           L0_ffn_w3, L0_ffn_sw1, L0_ffn_sw2, L0_ffn_sw3)
    nc = build_program(cfg)
    res = run_bass_kernel_spmd(nc, in_maps, core_ids=list(range(cfg["R"])))
    out = np.concatenate([np.asarray(r["out"]) for r in res.results], axis=0)
    return out.reshape(1, cfg["R"] * cfg["T"], cfg["D"]).astype(np.float32)
